# version 12
# baseline (speedup 1.0000x reference)
import numpy as np
import concourse.bass as bass
import concourse.mybir as mybir
from concourse.bass_utils import run_bass_kernel_spmd
from contextlib import ExitStack

F32 = mybir.dt.float32
F16 = mybir.dt.float16
BF16 = mybir.dt.bfloat16
ACT = mybir.ActivationFunctionType
ALU = mybir.AluOpType
AX = mybir.AxisListType

EPS = 1e-5
NCORES = 8
B, N, D = 32, 2048, 1024
SEQ_PER_CORE = B // NCORES
T = SEQ_PER_CORE * N
H, HD, F = 16, 64, 128
I = 2816
NCH = T // 128
CH_SEQ = N // 128
ND = D // 128
NI = I // 128
IPANEL = I // 2
NIP = IPANEL // 128

ENGINES = ("sync", "scalar", "vector", "tensor", "gpsimd")
BUILD_STAGES = "full"
DETECT_RACES = False


class Sched:

    def __init__(self):
        self.ops = []
        self.counts = {}
        self.waitmax = {}

    def __init_extra(self):
        pass

    def emit(self, engine, fn, waits=(), inc=None, step=1):
        if inc == "ev":
            inc = "ev_" + engine
        wm = self.waitmax.setdefault(engine, {})
        merged = {}
        for w in waits:
            if w is None:
                continue
            sem, val = w
            if (engine != "gpsimd" and sem == "ev_" + engine
                    and val <= self.counts.get(sem, 0)):
                continue
            if val <= wm.get(sem, 0):
                continue
            if val > merged.get(sem, 0):
                merged[sem] = val
        ws = tuple(merged.items())
        for sem, val in ws:
            wm[sem] = val
        val = None
        if inc is not None:
            val = self.counts.get(inc, 0) + step
            self.counts[inc] = val
        self.ops.append((engine, ws, fn, inc, step))
        return (inc, val) if inc is not None else None

    def last(self, engine):
        key = "ev_" + engine
        return (key, self.counts.get(key, 0))

    def barrier(self):
        targets = []
        for e in ENGINES:
            key = "bar_" + e
            self.emit(e, None, inc=key)
            targets.append((key, self.counts[key]))
        for e in ENGINES:
            self.emit(e, None, waits=targets)

    def run(self, eng, engine_name, sems):
        for engine, waits, fn, inc, step in self.ops:
            if engine != engine_name:
                continue
            for sem_key, val in waits:
                eng.wait_ge(sems[sem_key], val)
            if fn is None:
                if inc is not None:
                    eng.sem_inc(sems[inc], step)
                continue
            instr = fn(eng)
            if inc is not None:
                if instr is not None:
                    instr.then_inc(sems[inc], step)
                else:
                    eng.sem_inc(sems[inc], step)


class Guard:

    def __init__(self):
        self.ev = {}

    def wait(self, *names):
        return [self.ev[n] for n in names if n in self.ev]

    def set(self, name, event):
        if event is not None:
            self.ev[name] = event


def build_core_program():
    nc = bass.Bass("TRN2", target_bir_lowering=False, debug=False,
                   detect_race_conditions=DETECT_RACES)

    x_in = nc.dram_tensor("x", [T, D], F32, kind="ExternalInput").ap()
    wqkv_in = nc.dram_tensor("W_qkv", [3 * D, D], F32, kind="ExternalInput").ap()
    proj_in = nc.dram_tensor("proj", [H * HD, F], F32, kind="ExternalInput").ap()
    wout_in = nc.dram_tensor("W_out", [D, D], F32, kind="ExternalInput").ap()
    bout_in = nc.dram_tensor("b_out", [1, D], F32, kind="ExternalInput").ap()
    wg_in = nc.dram_tensor("w_gate", [I, D], F32, kind="ExternalInput").ap()
    wu_in = nc.dram_tensor("w_up", [I, D], F32, kind="ExternalInput").ap()
    wd_in = nc.dram_tensor("w_down", [D, I], F32, kind="ExternalInput").ap()
    y_out = nc.dram_tensor("y", [T, D], F32, kind="ExternalOutput").ap()

    s_qkvT = nc.dram_tensor("s_qkvT", [ND, 128, 3 * D], BF16, kind="Internal").ap()
    s_outT = nc.dram_tensor("s_outT", [ND, 128, D], BF16, kind="Internal").ap()
    s_gT = nc.dram_tensor("s_gT", [ND, 128, I], BF16, kind="Internal").ap()
    s_uT = nc.dram_tensor("s_uT", [ND, 128, I], BF16, kind="Internal").ap()
    s_dT = nc.dram_tensor("s_dT", [NI, 128, D], BF16, kind="Internal").ap()
    s_x1 = nc.dram_tensor("s_x1", [NCH, 128, D], F32, kind="Internal").ap()
    s_h2q = nc.dram_tensor("s_h2q", [NCH, ND, 128, 128], BF16, kind="Internal").ap()
    s_part = nc.dram_tensor("s_part", [NCH, 128, D], F32, kind="Internal").ap()

    sch = Sched()
    G = Guard()
    esg = ExitStack()

    def sb(stack, name, shape, dt):
        return stack.enter_context(nc.sbuf_tensor(name, shape, dt))

    def ps(stack, name, shape, dt=F32):
        return stack.enter_context(nc.psum_tensor(name, shape, dt))

    ident = sb(esg, "ident", [128, 128], BF16)
    ones_bf = sb(esg, "ones_bf", [1, 128], BF16)
    projf = sb(esg, "projf", [128, ND * F], BF16)
    bout_b = sb(esg, "bout_b", [1, D], BF16)
    kv_sb = sb(esg, "kv_sb", [128, SEQ_PER_CORE * H * 65], BF16)
    ceps = sb(esg, "ceps", [128, 1], F32)
    c30 = sb(esg, "c30", [128, 1], F32)
    cln6 = sb(esg, "cln6", [128, 1], F32)
    pf = [ps(esg, f"pf{i}", [128, 512], F32) for i in range(6)]
    ptb = [ps(esg, f"ptb{i}", [128, 128], BF16) for i in range(2)]

    e_ms = sch.emit("gpsimd", lambda g: g.memset(ident[:, :], 0.0), inc="ev")
    sch.emit("gpsimd", lambda g: g.affine_select(
        out=ident[:, :], in_=ident[:, :], compare_op=ALU.not_equal,
        fill=1.0, base=0, pattern=[[-1, 128]], channel_multiplier=1),
        waits=[e_ms])
    sch.emit("gpsimd", lambda g: g.memset(ones_bf[0:1, :], 1.0))
    sch.emit("gpsimd", lambda g: g.memset(c30[:, :], 1e-30))
    sch.emit("gpsimd", lambda g: g.memset(cln6[:, :], float(np.log(6.0))))
    e_c = sch.emit("gpsimd", lambda g: g.memset(ceps[:, :], EPS), inc="ev")
    def ln_chain(x_ap, out_ap, lb, width, deps, out_war=()):
        e1 = sch.emit("scalar", lambda s: s.activation(
            out=lb["sq"][:, :width], in_=x_ap, func=ACT.Copy,
            accum_out=lb["sum1"][:, :]), waits=list(deps), inc="ev")
        e2 = sch.emit("vector", lambda v: v.tensor_single_scalar(
            out=lb["nmean"][:, :], in_=lb["sum1"][:, :], scalar=-1.0 / width,
            op=ALU.mult), waits=[e1], inc="ev")
        sch.emit("scalar", lambda s: s.activation(
            out=lb["sq"][:, :width], in_=x_ap, func=ACT.Square,
            bias=lb["nmean"][:, 0:1], accum_out=lb["varsum"][:, :]), waits=[e2])
        sch.emit("scalar", lambda s: s.activation(
            out=lb["lnv"][:, :], in_=lb["varsum"][:, :], func=ACT.Ln,
            scale=1.0 / width, bias=ceps[:, 0:1]))
        e4 = sch.emit("scalar", lambda s: s.activation(
            out=lb["rstd"][:, :], in_=lb["lnv"][:, :], func=ACT.Exp,
            scale=-0.5), inc="ev")
        e5 = sch.emit("vector", lambda v: v.scalar_tensor_tensor(
            out=lb["nmr"][:, :], in0=lb["sum1"][:, :], scalar=-1.0 / width,
            in1=lb["rstd"][:, :], op0=ALU.mult, op1=ALU.mult),
            waits=[e4], inc="ev")
        e6 = sch.emit("scalar", lambda s: s.activation(
            out=out_ap, in_=x_ap, func=ACT.Identity,
            bias=lb["nmr"][:, 0:1], scale=lb["rstd"][:, 0:1]),
            waits=[e5] + list(out_war), inc="ev")
        return e6

    THRS = ((0.75, 1.0), (1.25, 1.0), (1.75, 1.0), (2.5, 2.0), (3.5, 2.0), (5.0, 4.0))

    def quant_front(x_ap, qb, width, ptag, deps):
        nb = width // 16
        e_a = sch.emit("scalar", lambda s: s.activation(
            out=qb["a"][:, :width], in_=x_ap, func=ACT.Abs),
            waits=list(deps) + G.wait(ptag + "_a"), inc="ev")
        e_sm = sch.emit("vector", lambda v: v.tensor_reduce(
            out=qb["smax"][:, :nb],
            in_=qb["a"][:, :width].rearrange("p (b k) -> p b k", k=16),
            axis=AX.X, op=ALU.max),
            waits=[e_a] + G.wait(ptag + "_smax"), inc="ev")
        sch.emit("scalar", lambda s: s.activation(
            out=qb["lns"][:, :nb], in_=qb["smax"][:, :nb], func=ACT.Ln,
            bias=c30[:, 0:1]), waits=[e_sm])
        e_rc = sch.emit("scalar", lambda s: s.activation(
            out=qb["rcp0"][:, :nb], in_=qb["lns"][:, :nb], func=ACT.Exp,
            scale=-1.0, bias=cln6[:, 0:1]),
            waits=G.wait(ptag + "_rcp0"), inc="ev")
        G.set(ptag + "_smax", e_rc)
        return e_rc

    def quant_back(x_ap, out_ap, qb, width, ptag, stag, e_rc, out_war=()):
        nb = width // 16
        rcp_b = qb["rcp0"][:, :nb].unsqueeze(2).broadcast_to([128, nb, 16])
        sh_b = qb["shv"][:, :nb].unsqueeze(2).broadcast_to([128, nb, 16])
        e_sh = sch.emit("vector", lambda v: v.tensor_scalar(
            out=qb["shv"][:, :nb], in0=qb["smax"][:, :nb], scalar1=1.0 / 12.0,
            scalar2=5e-13, op0=ALU.mult, op1=ALU.max), inc="ev")
        e_xn = sch.emit("vector", lambda v: v.tensor_tensor(
            out=qb["xn"][:, :width].rearrange("p (b k) -> p b k", k=16),
            in0=qb["a"][:, :width].rearrange("p (b k) -> p b k", k=16),
            in1=rcp_b, op=ALU.mult), waits=[e_rc], inc="ev")
        G.set(ptag + "_a", e_xn)
        G.set(ptag + "_rcp0", e_xn)
        e_sg = sch.emit("scalar", lambda s: s.activation(
            out=qb["sg"][:, :width], in_=x_ap, func=ACT.Sign),
            waits=G.wait(stag + "_sg"), inc="ev")
        names = ["acc", "acc2"]
        cur = 0
        sch.emit("vector", lambda v: v.tensor_scalar(
            out=qb["acc"][:, :width], in0=qb["xn"][:, :width], scalar1=0.25,
            scalar2=1.0, op0=ALU.is_gt, op1=ALU.mult))
        for thr, w in THRS:
            sch.emit("vector", lambda v, thr=thr, w=w: v.tensor_scalar(
                out=qb["m"][:, :width], in0=qb["xn"][:, :width],
                scalar1=thr, scalar2=w, op0=ALU.is_gt, op1=ALU.mult))
            dst, src = names[1 - cur], names[cur]
            sch.emit("vector", lambda v, dst=dst, src=src:
                     v.tensor_tensor(
                         out=qb[dst][:, :width], in0=qb["m"][:, :width],
                         in1=qb[src][:, :width], op=ALU.add))
            cur = 1 - cur
        e_ssg = sch.emit("vector", lambda v: v.tensor_tensor(
            out=qb["ssg"][:, :width].rearrange("p (b k) -> p b k", k=16),
            in0=qb["sg"][:, :width].rearrange("p (b k) -> p b k", k=16),
            in1=sh_b, op=ALU.mult), waits=[e_sg, e_sh], inc="ev")
        G.set(stag + "_sg", e_ssg)
        e_q = sch.emit("vector", lambda v, src=names[cur]: v.tensor_tensor(
            out=out_ap, in0=qb[src][:, :width], in1=qb["ssg"][:, :width],
            op=ALU.mult), waits=list(out_war), inc="ev")
        return e_q, e_sg

    def quant_chain(x_ap, out_ap, qb, width, tag, deps, out_war=()):
        e_rc = quant_front(x_ap, qb, width, tag, deps)
        e_q, _ = quant_back(x_ap, out_ap, qb, width, tag, tag, e_rc, out_war)
        return e_q

    def elu1(psum_ap, tmp_f32, tmp_e, out_ap, deps, war=(), eltag=None):
        e1 = sch.emit("vector", lambda v: v.tensor_single_scalar(
            out=tmp_f32, in_=psum_ap, scalar=0.0, op=ALU.min),
            waits=list(deps) + list(war)
            + (G.wait(eltag) if eltag else []), inc="ev")
        e2 = sch.emit("scalar", lambda s: s.activation(
            out=tmp_e, in_=tmp_f32, func=ACT.Exp), waits=[e1], inc="ev")
        e3 = sch.emit("vector", lambda v: v.scalar_tensor_tensor(
            out=out_ap, in0=psum_ap, scalar=0.0, in1=tmp_e,
            op0=ALU.max, op1=ALU.add), waits=[e2], inc="ev")
        if eltag:
            G.set(eltag, e2)
        return e3

    es0 = ExitStack()
    projf32 = sb(es0, "projf32", [128, ND * F], F32)
    bout_f = sb(es0, "bout_f", [1, D], F32)
    e_ps = []
    for j in range(ND):
        e_ps.append(sch.emit("sync", lambda s, j=j: s.dma_start(
            out=projf32[:, j * F:(j + 1) * F], in_=proj_in[j * 128:(j + 1) * 128, :]),
            inc="dv", step=16))
    e_b = sch.emit("sync", lambda s: s.dma_start(out=bout_f[0:1, :], in_=bout_in[0:1, :]), inc="dv", step=16)
    sch.emit("scalar", lambda s: s.activation(out=projf[:, :], in_=projf32[:, :], func=ACT.Copy),
             waits=e_ps + [e_b, e_c])
    sch.emit("scalar", lambda s: s.activation(out=bout_b[0:1, :], in_=bout_f[0:1, :], func=ACT.Copy))
    def make_qset(stack, tag, width, site16, depth=2):
        dt2 = F16 if site16 else BF16
        shared = {"xn": sb(stack, f"q{tag}_xn", [128, width], F16 if site16 else F32),
                  "ssg": sb(stack, f"q{tag}_ssg", [128, width], dt2),
                  "sg": sb(stack, f"q{tag}_sg", [128, width], dt2),
                  "shv": sb(stack, f"q{tag}_shv", [128, width // 16], dt2 if site16 else BF16)}
        for nm in ("acc", "acc2", "m"):
            shared[nm] = sb(stack, f"q{tag}_" + nm, [128, width], dt2)
        if depth == 1:
            shared["a"] = sb(stack, f"q{tag}_a", [128, width], F32)
            for nm in ("smax", "lns", "rcp0"):
                shared[nm] = sb(stack, f"q{tag}_" + nm, [128, width // 16], F32)
            return shared
        group = []
        for i in range(depth):
            qq = dict(shared)
            qq["a"] = sb(stack, f"q{tag}{i}_a", [128, width], F32)
            for nm in ("smax", "lns", "rcp0"):
                qq[nm] = sb(stack, f"q{tag}{i}_" + nm, [128, width // 16], F32)
            group.append(qq)
        return group

    w_nat = [sb(es0, f"w_nat{i}", [128, I], F32) for i in range(2)]
    w_b16 = [sb(es0, f"w_b16{i}", [128, I], BF16) for i in range(2)]
    pc_tr = ptb
    tr_st = [sb(es0, f"tr_st{i}", [128, NI * 128], BF16) for i in range(2)]
    q0 = make_qset(es0, "0", I, False)

    def s0_process(src_fn, n_chunks, chunk_w, do_quant, dst_fn):
        nj = chunk_w // 128

        def s0_tr(c, bi, e_cast):
            last_tr = None
            last_cp = None
            for j in range(nj):
                pj = j % 2
                e_tr = sch.emit("tensor", lambda t, bi=bi, j=j, pj=pj: t.transpose(
                    pc_tr[pj][:, :], w_b16[bi][:, j * 128:(j + 1) * 128],
                    ident[:, :]), waits=[e_cast] + G.wait(f"ptb{pj}"),
                    inc="ev")
                last_tr = e_tr
                e_ev = sch.emit("scalar", lambda s, j=j, pj=pj, bi=bi: s.activation(
                    out=tr_st[bi][:, j * 128:(j + 1) * 128], in_=pc_tr[pj][:, :],
                    func=ACT.Copy),
                    waits=[e_tr] + (G.wait(f"tr_st{bi}") if j == 0 else []),
                    inc="ev")
                G.set(f"ptb{pj}", e_ev)
                last_cp = e_ev
            e_d = sch.emit("gpsimd", lambda g, c=c, bi=bi, nj=nj: g.dma_start(
                out=dst_fn(c, nj),
                in_=tr_st[bi][:, :nj * 128].rearrange("p (j c) -> p j c", c=128)),
                waits=[last_cp], inc="dv2", step=16)
            G.set(f"tr_st{bi}", e_d)
            G.set(f"w_b16{bi}", last_tr)

        def s0_back(c, bi, e_rc):
            e_cast, e_sg = quant_back(w_nat[bi][:, :chunk_w],
                                      w_b16[bi][:, :chunk_w], q0[bi], chunk_w,
                                      f"q0{bi}", "q0", e_rc,
                                      out_war=G.wait(f"w_b16{bi}"))
            G.set(f"w_nat{bi}", e_sg)
            s0_tr(c, bi, e_cast)

        pend = None
        for c in range(n_chunks):
            bi = c % 2
            e_ld = sch.emit("sync", lambda s, c=c, bi=bi: s.dma_start(
                out=w_nat[bi][:, :chunk_w], in_=src_fn(c)),
                waits=G.wait(f"w_nat{bi}"), inc="dv", step=16)
            if do_quant:
                e_rc = quant_front(w_nat[bi][:, :chunk_w], q0[bi], chunk_w,
                                   f"q0{bi}", [e_ld])
                if pend is not None:
                    s0_back(*pend)
                pend = (c, bi, e_rc)
            else:
                e_cast = sch.emit("scalar", lambda s, bi=bi: s.activation(
                    out=w_b16[bi][:, :chunk_w], in_=w_nat[bi][:, :chunk_w],
                    func=ACT.Copy), waits=[e_ld] + G.wait(f"w_b16{bi}"),
                    inc="ev")
                G.set(f"w_nat{bi}", e_cast)
                s0_tr(c, bi, e_cast)
        if pend is not None:
            s0_back(*pend)

    import os
    _S0SEL = globals().get("S0SEL", "all")
    _S0N = globals().get("S0_NCHUNKS", 24)
    if _S0SEL in ("all", "plain"):
        s0_process(lambda c: wqkv_in[c * 128:(c + 1) * 128, :], _S0N, D, False,
               lambda c, nj: s_qkvT[:, :, c * 128:(c + 1) * 128].rearrange("j p c -> p j c"))
    if _S0SEL == "all":
        s0_process(lambda c: wout_in[c * 128:(c + 1) * 128, :], ND, D, False,
               lambda c, nj: s_outT[:, :, c * 128:(c + 1) * 128].rearrange("j p c -> p j c"))
    if _S0SEL in ("all", "quant"):
        s0_process(lambda c: wg_in[c * 128:(c + 1) * 128, :], NI, D, True,
               lambda c, nj: s_gT[:, :, c * 128:(c + 1) * 128].rearrange("j p c -> p j c"))
    if _S0SEL == "all":
        s0_process(lambda c: wu_in[c * 128:(c + 1) * 128, :], NI, D, True,
               lambda c, nj: s_uT[:, :, c * 128:(c + 1) * 128].rearrange("j p c -> p j c"))
        s0_process(lambda c: wd_in[c * 128:(c + 1) * 128, :], ND, I, True,
               lambda c, nj: s_dT[:, :, c * 128:(c + 1) * 128].rearrange("j p c -> p j c"))

    sch.barrier()
    es0.close()
    if BUILD_STAGES == "s0":
        sch.emit("gpsimd", lambda g: g.dma_start(out=y_out[0:128, 0:64],
                 in_=ident[:, :].bitcast(F32)[:, 0:64]), inc="dv2", step=16)
        sch.emit("gpsimd", None, waits=[("dv2", sch.counts["dv2"])])
        return _finish(nc, sch, esg)

    es13 = ExitStack()
    ln1_fm = sb(es13, "ln1_fm", [128, ND * N], BF16)
    xch = [sb(es13, f"xch{i}", [128, D], F32) for i in range(2)]

    lnb_sq = sb(es13, "lnb_sq", [128, D], F32)

    def make_lnb(tag):
        lb = {nm: sb(es13, f"lnb{tag}_" + nm, [128, 1], F32)
              for nm in ("sum1", "varsum", "nmean", "lnv", "rstd", "nmr")}
        lb["sq"] = lnb_sq
        return lb

    lnb = [make_lnb(0), make_lnb(1)]
    wA_sb = sb(es13, "wA_sb", [128, ND * D], BF16)
    wB_sb = sb(es13, "wB_sb", [128, ND * D], BF16)
    k_fm = [sb(es13, f"k_fm{i}", [128, N], BF16) for i in range(2)]
    v_tm = sb(es13, "v_tm", [128, CH_SEQ * H * 65], BF16)
    kp_ch = [sb(es13, f"kp_ch{i}", [128, 512], BF16) for i in range(2)]
    el_t = [sb(es13, f"el_t{i}", [128, 512], F32) for i in range(2)]
    el_e = [sb(es13, f"el_e{i}", [128, 512], F32) for i in range(2)]
    q_fm = [sb(es13, "q_fm0", [128, ND * 512], BF16)] * 2
    qp_t = [sb(es13, f"qp_t{i}", [128, 512], BF16) for i in range(2)]
    attn_fm = [sb(es13, "attn_fm0", [128, ND * 512], BF16)] * 2
    zpool = sb(es13, "zpool", [128, 512], F32)
    zbf = [zpool[0:1, 0:256].bitcast(BF16),
           zpool[0:1, 256:512].bitcast(BF16)]
    zden = [zpool[32:33, :], zpool[64:65, :]]
    zln = zpool[96:97, :]
    zbc_sb = [sb(es13, f"zbc_sb{i}", [128, 512], BF16) for i in range(2)]
    x1_sb = [sb(es13, f"x1_sb{i}", [128, D], F32) for i in range(2)]
    h2_sb = [sb(es13, "h2_sb0", [128, D], F32)] * 2
    h2q_tm = [sb(es13, f"h2q_tm{i}", [128, D], BF16) for i in range(4)]
    ln1_tm = h2q_tm[0:2]
    h2q_st = [sb(es13, f"h2q_st{i}", [128, D], BF16) for i in range(2)]

    q3 = make_qset(es13, "3", D, False, depth=1)

    for s in range(SEQ_PER_CORE):
        pc_t1 = ptb
        for c in range(CH_SEQ):
            bi = c % 2
            gc = s * CH_SEQ + c
            e_ld = sch.emit("sync", lambda e, gc=gc, bi=bi: e.dma_start(
                out=xch[bi][:, :], in_=x_in[gc * 128:(gc + 1) * 128, :]),
                waits=G.wait(f"xch{bi}"), inc="dv", step=16)
            e_ln = ln_chain(xch[bi][:, :], ln1_tm[bi][:, :], lnb[bi], D, [e_ld],
                            out_war=G.wait(f"ln1_tm{bi}"))
            G.set(f"xch{bi}", e_ln)
            last = None
            for j in range(ND):
                pj = j % 2
                e_tr = sch.emit("tensor", lambda t, j=j, pj=pj, bi=bi: t.transpose(
                    pc_t1[pj][:, :], ln1_tm[bi][:, j * 128:(j + 1) * 128],
                    ident[:, :]), waits=[e_ln] + G.wait(f"ptb{pj}"),
                    inc="ev")
                e_ev = sch.emit("scalar", lambda sc, j=j, pj=pj, c=c: sc.activation(
                    out=ln1_fm[:, j * N + c * 128: j * N + (c + 1) * 128],
                    in_=pc_t1[pj][:, :], func=ACT.Copy),
                    waits=[e_tr], inc="ev")
                G.set(f"ptb{pj}", e_ev)
                last = e_tr
            G.set(f"ln1_tm{bi}", last)
        sch.barrier()
        if BUILD_STAGES == "s1":
            sch.emit("gpsimd", lambda g: g.dma_start(out=y_out[0:128, :],
                     in_=ln1_fm[:, 0:D]), inc="dv2", step=16)
            sch.emit("gpsimd", None, waits=[("dv2", sch.counts["dv2"])])
            es13.close()
            return _finish(nc, sch, esg)

        pc_uv = pf[0:2]
        pc_ke = pf[2:4]
        pc_kv = pf[4][:, 0:65]

        ldk = []
        for j in range(ND):
            ldk.append(sch.emit("sync", lambda e, j=j: e.dma_start(
                out=wA_sb[:, j * D:(j + 1) * D], in_=s_qkvT[j, :, D:2 * D]),
                inc="dv", step=16))
            ldk.append(sch.emit("sync", lambda e, j=j: e.dma_start(
                out=wB_sb[:, j * D:(j + 1) * D], in_=s_qkvT[j, :, 2 * D:3 * D]),
                inc="dv", step=16))
        e_vones = sch.emit("vector", lambda v: v.memset(
            v_tm[:, :].rearrange("p (c h k) -> p (c h) k", h=H, k=65)[:, :, 64:65],
            1.0), inc="ev")

        vt_ev = {}
        kf_ev = {}
        uvp = [0]

        def vt_unit(c, half, ldk=ldk):
            pi = uvp[0] % 2
            uvp[0] += 1
            for j in range(ND):
                sch.emit("tensor", lambda t, c=c, half=half, j=j, pi=pi: t.matmul(
                    pc_uv[pi][:, :],
                    ln1_fm[:, j * N + c * 128: j * N + (c + 1) * 128],
                    wB_sb[:, j * D + half * 512: j * D + (half + 1) * 512],
                    start=(j == 0), stop=(j == ND - 1)),
                    waits=(ldk + G.wait(f"pf{pi}") if j == 0 else []),
                    inc=("ev" if j == ND - 1 else None))
            e_mm = sch.last("tensor")
            e_ev = sch.emit("scalar", lambda sc, c=c, half=half, pi=pi: sc.activation(
                out=v_tm[:, c * H * 65 + half * 8 * 65: c * H * 65 + (half * 8 + 8) * 65]
                    .rearrange("p (h k) -> p h k", k=65)[:, :, 0:64],
                in_=pc_uv[pi][:, :].rearrange("p (h k) -> p h k", k=64),
                func=ACT.Copy), waits=[e_mm], inc="ev")
            G.set(f"pf{pi}", e_ev)
            vt_ev[(c, half)] = e_ev

        def kf_unit(oc, tt, ldk=ldk):
            pi = uvp[0] % 2
            uvp[0] += 1
            kbuf = k_fm[oc % 2]
            for j in range(ND):
                sch.emit("tensor", lambda t, oc=oc, tt=tt, j=j, pi=pi: t.matmul(
                    pc_uv[pi][:, :],
                    wA_sb[:, j * D + oc * 128: j * D + (oc + 1) * 128],
                    ln1_fm[:, j * N + tt * 512:j * N + (tt + 1) * 512],
                    start=(j == 0), stop=(j == ND - 1)),
                    waits=(ldk + G.wait(f"pf{pi}", f"k_fm{oc % 2}") if j == 0 else []),
                    inc=("ev" if j == ND - 1 else None))
            e_mm = sch.last("tensor")
            e_ev = sch.emit("scalar", lambda sc, tt=tt, pi=pi, kbuf=kbuf: sc.activation(
                out=kbuf[:, tt * 512:(tt + 1) * 512],
                in_=pc_uv[pi][:, :], func=ACT.Copy), waits=[e_mm], inc="ev")
            G.set(f"pf{pi}", e_ev)
            kf_ev[(oc, tt)] = e_ev

        kep = [0]

        def ke_unit_a(oc, hh, g):
            pi = kep[0] % 2
            kep[0] += 1
            kbuf = k_fm[oc % 2]
            poff = hh * 64
            for ci in range(4):
                c = g * 4 + ci
                sch.emit("tensor", lambda t, c=c, ci=ci, pi=pi, poff=poff, oc=oc, kbuf=kbuf: t.matmul(
                    pc_ke[pi][:, ci * 128:(ci + 1) * 128],
                    kbuf[poff:poff + 64, c * 128:(c + 1) * 128],
                    projf[poff:poff + 64, oc * F:(oc + 1) * F],
                    start=True, stop=True, skip_group_check=True),
                    waits=([kf_ev[(oc, g)]] + G.wait(f"pf{2+pi}") if ci == 0 else []),
                    inc=("ev" if ci == 3 else None))
            e_ke = sch.last("tensor")
            e1 = sch.emit("vector", lambda v, pi=pi: v.tensor_single_scalar(
                out=el_t[pi][:, :], in_=pc_ke[pi][:, :], scalar=0.0,
                op=ALU.min), waits=[e_ke] + G.wait(f"el{pi}"), inc="ev")
            e2 = sch.emit("scalar", lambda s, pi=pi: s.activation(
                out=el_e[pi][:, :], in_=el_t[pi][:, :], func=ACT.Exp),
                waits=[e1], inc="ev")
            G.set(f"el{pi}", e2)
            return (pi, e2, e_ke)

        def ke_unit_b(pi, e2):
            e_kp = sch.emit("vector", lambda v, pi=pi: v.scalar_tensor_tensor(
                out=kp_ch[pi][:, :], in0=pc_ke[pi][:, :], scalar=0.0,
                in1=el_e[pi][:, :], op0=ALU.max, op1=ALU.add),
                waits=[e2] + G.wait(f"kp_ch{pi}"), inc="ev")
            G.set(f"pf{2+pi}", e_kp)
            return e_kp

        def kv_unit(oc, hh, g, pi, e_kp, s=s):
            h = 2 * oc + hh
            half = h // 8
            for ci in range(4):
                c = g * 4 + ci
                w0 = []
                if ci == 0:
                    w0 = [e_kp] + [vt_ev[(g * 4 + c2, half)] for c2 in range(4)]
                    if c == 0:
                        w0 += [e_vones] + G.wait("pf4")
                sch.emit("tensor", lambda t, h=h, c=c, ci=ci, pi=pi: t.matmul(
                    pc_kv[:, :], kp_ch[pi][:, ci * 128:(ci + 1) * 128],
                    v_tm[:, c * H * 65 + h * 65: c * H * 65 + (h + 1) * 65],
                    start=(c == 0), stop=(c == CH_SEQ - 1),
                    skip_group_check=True),
                    waits=w0, inc=("ev" if ci == 3 else None))
            G.set(f"kp_ch{pi}", sch.last("tensor"))
            if g == CH_SEQ // 4 - 1:
                e_kv = sch.last("tensor")
                e_kve = sch.emit("scalar", lambda sc, s=s, h=h: sc.activation(
                    out=kv_sb[:, (s * H + h) * 65:(s * H + h + 1) * 65],
                    in_=pc_kv[:, :], func=ACT.Copy), waits=[e_kv], inc="ev")
                G.set("pf4", e_kve)

        for tt in range(4):
            kf_unit(0, tt)
        for c in range(CH_SEQ):
            vt_unit(c, 0)
        for oc in range(ND):
            if oc < 4:
                fill = ([("kf", oc + 1, t) for t in range(4)]
                        + [("vt", 4 * oc + i, 1) for i in range(4)])
            elif oc < ND - 1:
                fill = [("kf", oc + 1, t) for t in range(4)]
            else:
                fill = []
            units = [(hh, g) for hh in range(2) for g in range(CH_SEQ // 4)]
            fi = 0
            pend_a = None
            pend_b = None
            last_ke = None
            for idx in range(len(units) + 2):
                if idx < len(units):
                    hh, g = units[idx]
                    pi, e2, e_ke = ke_unit_a(oc, hh, g)
                    last_ke = e_ke
                    cur_a = (hh, g, pi, e2)
                else:
                    cur_a = None
                if pend_a is not None:
                    ahh, ag, api, ae2 = pend_a
                    e_kp = ke_unit_b(api, ae2)
                    cur_b = (ahh, ag, api, e_kp)
                else:
                    cur_b = None
                if pend_b is not None:
                    kv_unit(oc, *pend_b)
                take = (len(fill) * min(idx + 1, len(units))) // len(units)
                while fi < take:
                    kind, a, b = fill[fi]
                    fi += 1
                    if kind == "kf":
                        kf_unit(a, b)
                    else:
                        vt_unit(a, b)
                pend_a = cur_a
                pend_b = cur_b
            G.set(f"k_fm{oc % 2}", last_ke)
        sch.barrier()
        if BUILD_STAGES == "s2":
            sch.emit("gpsimd", lambda g: g.dma_start(out=y_out[0:128, 0:1024],
                     in_=kv_sb[:, 0:1024]), inc="dv2", step=16)
            sch.emit("gpsimd", None, waits=[("dv2", sch.counts["dv2"])])
            es13.close()
            return _finish(nc, sch, esg)

        pc_q = pf[0]
        pc_qe = [pf[1], pf[0]]
        pc_a = [pf[2][0:65, :], pf[3][0:65, :]]
        pc_zb = pf[4][0:64, :]
        pc_o = pf[5]

        ldq = []
        for j in range(ND):
            ldq.append(sch.emit("sync", lambda e, j=j: e.dma_start(
                out=wA_sb[:, j * D:(j + 1) * D], in_=s_qkvT[j, :, 0:D]),
                inc="dv", step=16))
            ldq.append(sch.emit("sync", lambda e, j=j: e.dma_start(
                out=wB_sb[:, j * D:(j + 1) * D], in_=s_outT[j, :, :]),
                inc="dv", step=16))

        def s3_X(tt, s=s, ldq=ldq):
            ti = tt % 2
            qfm_ev = []
            for oc in range(ND):
                for j in range(ND):
                    sch.emit("tensor", lambda t, oc=oc, tt=tt, j=j: t.matmul(
                        pc_q[:, :],
                        wA_sb[:, j * D + oc * 128: j * D + (oc + 1) * 128],
                        ln1_fm[:, j * N + tt * 512:j * N + (tt + 1) * 512],
                        start=(j == 0), stop=(j == ND - 1)),
                        waits=(ldq + G.wait("pf0", "q_fmg") if j == 0 else []),
                        inc=("ev" if j == ND - 1 else None))
                e_mm = sch.last("tensor")
                e_ev = sch.emit("scalar", lambda sc, oc=oc, ti=ti: sc.activation(
                    out=q_fm[ti][:, oc * 512:(oc + 1) * 512], in_=pc_q[:, :],
                    func=ACT.Copy), waits=[e_mm], inc="ev")
                G.set("pf0", e_ev)
                qfm_ev.append(e_ev)

            hst = {}
            e_qe_last = [None]
            e_am_last = [None]

            def s3_hA1(h):
                pi = h % 2
                poff = (h % 2) * 64
                jc = h // 2
                qek = "pf1" if pi == 0 else "pf0"
                e_qe = sch.emit("tensor", lambda t, pi=pi, poff=poff, jc=jc, ti=ti: t.matmul(
                    pc_qe[pi][:, :],
                    projf[poff:poff + 64, jc * F:(jc + 1) * F],
                    q_fm[ti][poff:poff + 64, jc * 512:(jc + 1) * 512],
                    start=True, stop=True),
                    waits=[qfm_ev[jc]] + G.wait(qek), inc="ev")
                e_qe_last[0] = e_qe
                e1 = sch.emit("vector", lambda v, pi=pi: v.tensor_single_scalar(
                    out=el_t[pi][:, :], in_=pc_qe[pi][:, :], scalar=0.0,
                    op=ALU.min), waits=[e_qe] + G.wait(f"el{pi}"), inc="ev")
                e2 = sch.emit("scalar", lambda s, pi=pi: s.activation(
                    out=el_e[pi][:, :], in_=el_t[pi][:, :], func=ACT.Exp),
                    waits=[e1], inc="ev")
                G.set(f"el{pi}", e2)
                hst[("el", h)] = e2

            def s3_hA2(h):
                pi = h % 2
                qek = "pf1" if pi == 0 else "pf0"
                e2 = hst.pop(("el", h))
                e_qp = sch.emit("vector", lambda v, pi=pi: v.scalar_tensor_tensor(
                    out=qp_t[pi][:, :], in0=pc_qe[pi][:, :], scalar=0.0,
                    in1=el_e[pi][:, :], op0=ALU.max, op1=ALU.add),
                    waits=[e2] + G.wait(f"qp_t{pi}"), inc="ev")
                G.set(qek, e_qp)
                hst[("qp", h)] = e_qp

            def s3_hB1(h):
                pi = h % 2
                e_qp = hst.pop(("qp", h))
                e_at = sch.emit("tensor", lambda t, s=s, h=h, pi=pi: t.matmul(
                    pc_a[pi][:, :],
                    kv_sb[:, (s * H + h) * 65:(s * H + h + 1) * 65],
                    qp_t[pi][:, :], start=True, stop=True),
                    waits=[e_qp] + G.wait(f"pf{2+pi}"), inc="ev")
                G.set(f"qp_t{pi}", e_at)
                e_dn = sch.emit("vector", lambda v, pi=pi: v.tensor_single_scalar(
                    out=zden[pi][0:1, :], in_=pc_a[pi][64:65, :], scalar=1e-8,
                    op=ALU.add), waits=[e_at] + G.wait(f"zden{pi}"), inc="ev")
                hst[("dn", h)] = e_dn

            def s3_hB2(h):
                pi = h % 2
                e_dn = hst.pop(("dn", h))
                e_zl = sch.emit("scalar", lambda sc, pi=pi: sc.activation(
                    out=zln[0:1, :], in_=zden[pi][0:1, :], func=ACT.Ln),
                    waits=[e_dn], inc="ev")
                G.set(f"zden{pi}", e_zl)
                e_zb = sch.emit("scalar", lambda sc, pi=pi: sc.activation(
                    out=zbf[pi][0:1, :], in_=zln[0:1, :], func=ACT.Exp,
                    scale=-1.0), waits=G.wait(f"zbf{pi}"), inc="ev")
                hst[("zb", h)] = e_zb

            def s3_hC(h):
                pi = h % 2
                poff = (h % 2) * 64
                jc = h // 2
                e_zb = hst.pop(("zb", h))
                e_zm = sch.emit("tensor", lambda t, pi=pi: t.matmul(
                    pc_zb[:, :], ones_bf[0:1, 0:64], zbf[pi][0:1, :],
                    start=True, stop=True),
                    waits=[e_zb] + G.wait("pf4"), inc="ev")
                G.set(f"zbf{pi}", e_zm)
                e_ze = sch.emit("scalar", lambda sc, pi=pi: sc.activation(
                    out=zbc_sb[pi][0:64, :], in_=pc_zb[:, :], func=ACT.Copy),
                    waits=[e_zm] + G.wait(f"zbc_sb{pi}"), inc="ev")
                G.set("pf4", e_ze)
                e_am = sch.emit("vector", lambda v, pi=pi, poff=poff, jc=jc, ti=ti: v.tensor_tensor(
                    out=attn_fm[ti][poff:poff + 64, jc * 512:(jc + 1) * 512],
                    in0=pc_a[pi][0:64, :], in1=zbc_sb[pi][0:64, :], op=ALU.mult),
                    waits=[e_ze] + G.wait("attn_fmg"), inc="ev")
                G.set(f"pf{2+pi}", e_am)
                G.set(f"zbc_sb{pi}", e_am)
                e_am_last[0] = e_am

            for k in range(H + 4):
                if k < H:
                    s3_hA1(k)
                if 1 <= k <= H:
                    s3_hA2(k - 1)
                if k >= 4:
                    s3_hC(k - 4)
                if 2 <= k <= H + 1:
                    s3_hB1(k - 2)
                if 3 <= k <= H + 2:
                    s3_hB2(k - 3)
            G.set("q_fmg", e_qe_last[0])
            return e_am_last[0]

        def s3_P3b_chunk(tt, cc, s=s):
            gc = s * CH_SEQ + tt * 4 + cc
            pp = cc % 2
            e_h2q = st3.pop((tt, cc))
            last3 = None
            last_cp = None
            for j in range(ND):
                pj = j % 2
                e_tr = sch.emit("tensor", lambda t, j=j, pj=pj, cc=cc: t.transpose(
                    ptb[pj][:, :], h2q_tm[cc][:, j * 128:(j + 1) * 128],
                    ident[:, :]), waits=[e_h2q] + G.wait(f"ptb{pj}"),
                    inc="ev")
                e_ev2 = sch.emit("scalar", lambda sc, j=j, pj=pj, pp=pp: sc.activation(
                    out=h2q_st[pp][:, j * 128:(j + 1) * 128], in_=ptb[pj][:, :],
                    func=ACT.Copy),
                    waits=[e_tr] + (G.wait(f"h2q_st{pp}") if j == 0 else []),
                    inc="ev")
                G.set(f"ptb{pj}", e_ev2)
                last3 = e_tr
                last_cp = e_ev2
            e_d2 = sch.emit("gpsimd", lambda g, gc=gc, pp=pp: g.dma_start(
                out=s_h2q[gc, :, :, :].rearrange("j p c -> p j c"),
                in_=h2q_st[pp][:, :].rearrange("p (j c) -> p j c", c=128)),
                waits=[last_cp], inc="dv2", step=16)
            G.set(f"h2q_st{pp}", e_d2)
            G.set(f"h2q_tm{cc}", last3)

        def s3_P3a(tt, e_attn, prev_tt, s=s):
            ti = tt % 2
            last_wout = None
            for cc in range(4):
                if prev_tt is not None:
                    s3_P3b_chunk(prev_tt, cc)
                gc = s * CH_SEQ + tt * 4 + cc
                bi = cc % 2
                e_xl = sch.emit("sync", lambda e, gc=gc, bi=bi: e.dma_start(
                    out=xch[bi][:, :], in_=x_in[gc * 128:(gc + 1) * 128, :]),
                    waits=G.wait(f"xch{bi}"), inc="dv", step=16)
                e_x1_parts = []
                for half in range(2):
                    for j in range(ND):
                        sch.emit("tensor", lambda t, cc=cc, half=half, j=j, ti=ti: t.matmul(
                            pc_o[:, :],
                            attn_fm[ti][:, j * 512 + cc * 128: j * 512 + (cc + 1) * 128],
                            wB_sb[:, j * D + half * 512: j * D + (half + 1) * 512],
                            start=(j == 0), stop=False),
                            waits=([e_attn] + G.wait("pf5") if j == 0 else []))
                    sch.emit("tensor", lambda t, half=half: t.matmul(
                        pc_o[:, :], ones_bf[0:1, :],
                        bout_b[0:1, half * 512:(half + 1) * 512],
                        start=False, stop=True), inc="ev")
                    e_ob = sch.last("tensor")
                    e_tt = sch.emit("vector", lambda v, half=half, bi=bi: v.tensor_tensor(
                        out=x1_sb[bi][:, half * 512:(half + 1) * 512],
                        in0=pc_o[:, :],
                        in1=xch[bi][:, half * 512:(half + 1) * 512],
                        op=ALU.add),
                        waits=[e_ob, e_xl] + G.wait(f"x1_sb{bi}", f"x1d{bi}"),
                        inc="ev")
                    G.set("pf5", e_tt)
                    e_x1_parts.append(e_tt)
                last_wout = e_x1_parts[-1]
                G.set(f"xch{bi}", last_wout)
                e_x1d = sch.emit("gpsimd", lambda g, gc=gc, bi=bi: g.dma_start(
                    out=s_x1[gc, :, :], in_=x1_sb[bi][:, :]), waits=e_x1_parts,
                    inc="dv2", step=16)
                G.set(f"x1d{bi}", e_x1d)
                e_ln2 = ln_chain(x1_sb[bi][:, :], h2_sb[bi][:, :], lnb[bi], D,
                                 e_x1_parts)
                G.set(f"x1_sb{bi}", e_ln2)
                e_h2q = quant_chain(h2_sb[bi][:, :], h2q_tm[cc][:, :], q3, D,
                                    "q3", [e_ln2],
                                    out_war=G.wait(f"h2q_tm{cc}"))
                st3[(tt, cc)] = e_h2q
            G.set("attn_fmg", last_wout)

        st3 = {}
        e_attn = s3_X(0)
        s3_P3a(0, e_attn, None)
        for tt in range(1, 4):
            e_attn = s3_X(tt)
            s3_P3a(tt, e_attn, tt - 1)
        for cc in range(4):
            s3_P3b_chunk(3, cc)
        sch.barrier()
        G.ev.pop("q_fmg", None)
        if BUILD_STAGES == "s3":
            break
    es13.close()
    if BUILD_STAGES == "s3":
        sch.emit("gpsimd", None, waits=[("dv2", sch.counts["dv2"])])
        return _finish(nc, sch, esg)

    es4 = ExitStack()
    tg_sb = sb(es4, "tg_sb", [128, ND * IPANEL], BF16)
    tu_sb = sb(es4, "tu_sb", [128, ND * IPANEL], BF16)
    td_sb = sb(es4, "td_sb", [128, NIP * D], BF16)
    h2q_in = [sb(es4, f"h2q_in{i}", [128, ND * 128], BF16) for i in range(2)]
    res_in = [sb(es4, f"res_in{i}", [128, D], F32) for i in range(4)]
    h_sb = [sb(es4, f"h_sb{i}", [128, IPANEL], F32) for i in range(3)]
    silu_sb = [sb(es4, f"silu_sb{i}", [128, 512], F32) for i in range(2)]
    hq_tm = [sb(es4, f"hq_tm{i}", [128, IPANEL], BF16) for i in range(2)]
    hq_fm = [sb(es4, f"hq_fm{i}", [128, NIP * 128], BF16) for i in range(2)]
    out_st = [sb(es4, f"out_st{i}", [128, D], F32) for i in range(2)]

    q4 = make_qset(es4, "4", IPANEL, True, depth=3)
    pc_g = [pf[0], pf[2]]
    pc_u = [pf[1], pf[3]]
    pc_t4 = ptb
    pc_d = [pf[4], pf[5]]

    ITILES = (512, 512, 384)

    for panel in range(2):
        ip0 = panel * IPANEL
        ldw = []
        for j in range(ND):
            ldw.append(sch.emit("sync", lambda e, j=j, ip0=ip0: e.dma_start(
                out=tg_sb[:, j * IPANEL:(j + 1) * IPANEL],
                in_=s_gT[j, :, ip0:ip0 + IPANEL]), inc="dv", step=16))
            ldw.append(sch.emit("sync", lambda e, j=j, ip0=ip0: e.dma_start(
                out=tu_sb[:, j * IPANEL:(j + 1) * IPANEL],
                in_=s_uT[j, :, ip0:ip0 + IPANEL]), inc="dv", step=16))
        for i in range(NIP):
            ldw.append(sch.emit("sync", lambda e, i=i, panel=panel: e.dma_start(
                out=td_sb[:, i * D:(i + 1) * D],
                in_=s_dT[panel * NIP + i, :, :]), inc="dv", step=16))

        st4 = {}
        st4b = {}
        st4c = {}

        def s4_front(gc, panel=panel, ldw=ldw):
            bi = gc % 2
            qi = gc % 3
            ri = gc % 4
            e_hin = sch.emit("sync", lambda e, gc=gc, bi=bi: e.dma_start(
                out=h2q_in[bi][:, :].rearrange("p (j c) -> p j c", c=128),
                in_=s_h2q[gc, :, :, :].rearrange("j p c -> p j c")),
                waits=G.wait(f"h2q_in{bi}"), inc="dv", step=16)
            e_res = sch.emit("sync", lambda e, gc=gc, ri=ri, panel=panel: e.dma_start(
                out=res_in[ri][:, :],
                in_=(s_x1[gc, :, :] if panel == 0 else s_part[gc, :, :])),
                waits=G.wait(f"res_in{ri}"), inc="dv", step=16)
            off = 0
            e_h = None
            for it, itw in enumerate(ITILES):
                gi = bi
                for j in range(ND):
                    sch.emit("tensor", lambda t, j=j, bi=bi, off=off, itw=itw, gi=gi: t.matmul(
                        pc_g[gi][:, :itw], h2q_in[bi][:, j * 128:(j + 1) * 128],
                        tg_sb[:, j * IPANEL + off: j * IPANEL + off + itw],
                        start=(j == 0), stop=(j == ND - 1)),
                        waits=(ldw + [e_hin] + G.wait(f"pg{gi}") if j == 0 else []),
                        inc=("ev" if j == ND - 1 else None))
                e_g = sch.last("tensor")
                for j in range(ND):
                    sch.emit("tensor", lambda t, j=j, bi=bi, off=off, itw=itw, gi=gi: t.matmul(
                        pc_u[gi][:, :itw], h2q_in[bi][:, j * 128:(j + 1) * 128],
                        tu_sb[:, j * IPANEL + off: j * IPANEL + off + itw],
                        start=(j == 0), stop=(j == ND - 1)),
                        waits=(G.wait(f"pu{gi}") if j == 0 else []),
                        inc=("ev" if j == ND - 1 else None))
                e_u = sch.last("tensor")
                e_si = sch.emit("scalar", lambda sc, itw=itw, bi=bi, gi=gi: sc.activation(
                    out=silu_sb[bi][:, :itw], in_=pc_g[gi][:, :itw], func=ACT.Silu),
                    waits=[e_g] + G.wait(f"silu_sb{bi}"), inc="ev")
                G.set(f"pg{gi}", e_si)
                e_hm = sch.emit("vector", lambda v, off=off, itw=itw, bi=bi, qi=qi, gi=gi: v.tensor_tensor(
                    out=h_sb[qi][:, off:off + itw], in0=silu_sb[bi][:, :itw],
                    in1=pc_u[gi][:, :itw], op=ALU.mult),
                    waits=[e_si, e_u] + G.wait(f"h_sb{qi}"), inc="ev")
                G.set(f"pu{gi}", e_hm)
                G.set(f"silu_sb{bi}", e_hm)
                e_h = e_hm
                off += itw
            G.set(f"h2q_in{bi}", sch.last("tensor"))
            e_rc = quant_front(h_sb[qi][:, :], q4[qi], IPANEL, f"q4{qi}", [e_h])
            st4[gc] = (e_rc, e_res)

        def s4_back(gc):
            bi = gc % 2
            qi = gc % 3
            e_rc, e_res = st4.pop(gc)
            e_hq, _ = quant_back(h_sb[qi][:, :], hq_tm[bi][:, :], q4[qi],
                                 IPANEL, f"q4{qi}", "q4", e_rc,
                                 out_war=G.wait(f"hq_tm{bi}"))
            G.set(f"h_sb{qi}", e_hq)
            st4b[gc] = (e_hq, e_res)

        def s4_out_t(gc):
            bi = gc % 2
            e_hq, e_res = st4b.pop(gc)
            last_tr4 = None
            for i in range(NIP):
                pj = i % 2
                e_tr = sch.emit("tensor", lambda t, i=i, pj=pj, bi=bi: t.transpose(
                    pc_t4[pj][:, :], hq_tm[bi][:, i * 128:(i + 1) * 128],
                    ident[:, :]), waits=[e_hq] + G.wait(f"ptb{pj}"),
                    inc="ev")
                e_ev = sch.emit("scalar", lambda sc, i=i, pj=pj, bi=bi: sc.activation(
                    out=hq_fm[bi][:, i * 128:(i + 1) * 128], in_=pc_t4[pj][:, :],
                    func=ACT.Copy), waits=[e_tr] + G.wait(f"hq_fm{bi}"), inc="ev")
                G.set(f"ptb{pj}", e_ev)
                last_tr4 = e_tr
            G.set(f"hq_tm{bi}", last_tr4)
            e_hqf = sch.last("scalar")
            e_ds = []
            for half in range(2):
                for i in range(NIP):
                    sch.emit("tensor", lambda t, half=half, i=i, bi=bi: t.matmul(
                        pc_d[half][:, :], hq_fm[bi][:, i * 128:(i + 1) * 128],
                        td_sb[:, i * D + half * 512: i * D + (half + 1) * 512],
                        start=(i == 0), stop=(i == NIP - 1)),
                        waits=([e_hqf] + G.wait(f"pf{4+half}") if i == 0 else []),
                        inc=("ev" if i == NIP - 1 else None))
                e_ds.append(sch.last("tensor"))
            G.set(f"hq_fm{bi}", e_ds[-1])
            st4c[gc] = (e_ds, e_res)

        def s4_out_v(gc, panel=panel):
            bi = gc % 2
            ri = gc % 4
            e_ds, e_res = st4c.pop(gc)
            e_outs = []
            for half in range(2):
                e_o = sch.emit("vector", lambda v, half=half, bi=bi, ri=ri: v.tensor_tensor(
                    out=out_st[bi][:, half * 512:(half + 1) * 512],
                    in0=pc_d[half][:, :],
                    in1=res_in[ri][:, half * 512:(half + 1) * 512],
                    op=ALU.add),
                    waits=[e_ds[half], e_res] + G.wait(f"out_st{bi}"), inc="ev")
                G.set(f"pf{4+half}", e_o)
                e_outs.append(e_o)
            G.set(f"res_in{ri}", e_outs[-1])
            if panel == 0:
                e_od = sch.emit("gpsimd", lambda g, gc=gc, bi=bi: g.dma_start(
                    out=s_part[gc, :, :], in_=out_st[bi][:, :]),
                    waits=e_outs, inc="dv2", step=16)
            else:
                e_od = sch.emit("gpsimd", lambda g, gc=gc, bi=bi: g.dma_start(
                    out=y_out[gc * 128:(gc + 1) * 128, :], in_=out_st[bi][:, :]),
                    waits=e_outs, inc="dv2", step=16)
            G.set(f"out_st{bi}", e_od)

        for k in range(NCH + 3):
            if k >= 3:
                s4_out_t(k - 3)
            if 2 <= k < NCH + 2:
                s4_back(k - 2)
            if k >= 3:
                s4_out_v(k - 3)
            if k < NCH:
                s4_front(k)
        sch.barrier()

    sch.emit("gpsimd", None, waits=[("dv2", sch.counts["dv2"])])
    es4.close()

    return _finish(nc, sch, esg)


def _finish(nc, sch, esg):
    sems = {}
    sem_stack = ExitStack()
    for key in set(list(sch.counts.keys()) + ["dv", "dv2"]):
        sems[key] = sem_stack.enter_context(nc.semaphore(key))
    with nc.Block() as block:
        @block.sync
        def _(e):
            sch.run(e, "sync", sems)

        @block.scalar
        def _(e):
            sch.run(e, "scalar", sems)

        @block.vector
        def _(e):
            sch.run(e, "vector", sems)

        @block.tensor
        def _(e):
            sch.run(e, "tensor", sems)

        @block.gpsimd
        def _(e):
            sch.run(e, "gpsimd", sems)
    sem_stack.close()
    esg.close()
    return nc


_NC_CACHE = [None]


def kernel(L0_attn_x, L0_attn_W_qkv, L0_attn_proj, L0_attn_W_out, L0_attn_b_out,
           L0_ffn_w_gate, L0_ffn_w_up, L0_ffn_w_down):
    x = np.ascontiguousarray(np.asarray(L0_attn_x, dtype=np.float32))
    wqkv = np.ascontiguousarray(np.asarray(L0_attn_W_qkv, dtype=np.float32))
    proj = np.ascontiguousarray(np.asarray(L0_attn_proj, dtype=np.float32)).reshape(H * HD, F)
    wout = np.ascontiguousarray(np.asarray(L0_attn_W_out, dtype=np.float32))
    bout = np.ascontiguousarray(np.asarray(L0_attn_b_out, dtype=np.float32)).reshape(1, D)
    wg = np.ascontiguousarray(np.asarray(L0_ffn_w_gate, dtype=np.float32))
    wu = np.ascontiguousarray(np.asarray(L0_ffn_w_up, dtype=np.float32))
    wd = np.ascontiguousarray(np.asarray(L0_ffn_w_down, dtype=np.float32))

    if _NC_CACHE[0] is None:
        _NC_CACHE[0] = build_core_program()
    nc = _NC_CACHE[0]
    in_maps = []
    for c in range(NCORES):
        xs = x[c * SEQ_PER_CORE:(c + 1) * SEQ_PER_CORE].reshape(T, D)
        in_maps.append({"x": np.ascontiguousarray(xs), "W_qkv": wqkv, "proj": proj,
                        "W_out": wout, "b_out": bout, "w_gate": wg, "w_up": wu,
                        "w_down": wd})
    res = run_bass_kernel_spmd(nc, in_maps, core_ids=list(range(NCORES)))
    out = np.concatenate([r["y"].reshape(SEQ_PER_CORE, N, D) for r in res.results], axis=0)
    return out


# version 13 (final)
# speedup vs baseline: 1.1698x; 1.1698x over previous
import numpy as np
import concourse.bass as bass
import concourse.mybir as mybir
from concourse.bass_utils import run_bass_kernel_spmd
from contextlib import ExitStack

F32 = mybir.dt.float32
F16 = mybir.dt.float16
BF16 = mybir.dt.bfloat16
ACT = mybir.ActivationFunctionType
ALU = mybir.AluOpType
AX = mybir.AxisListType

EPS = 1e-5
NCORES = 8
B, N, D = 32, 2048, 1024
SEQ_PER_CORE = B // NCORES
T = SEQ_PER_CORE * N
H, HD, F = 16, 64, 128
I = 2816
NCH = T // 128
CH_SEQ = N // 128
ND = D // 128
NI = I // 128
IPANEL = I // 2
NIP = IPANEL // 128

ENGINES = ("sync", "scalar", "vector", "tensor", "gpsimd")
BUILD_STAGES = "full"
DETECT_RACES = False


class Sched:

    def __init__(self):
        self.ops = []
        self.counts = {}
        self.waitmax = {}

    def __init_extra(self):
        pass

    def emit(self, engine, fn, waits=(), inc=None, step=1):
        if inc == "ev":
            inc = "ev_" + engine
        wm = self.waitmax.setdefault(engine, {})
        merged = {}
        for w in waits:
            if w is None:
                continue
            sem, val = w
            if (engine != "gpsimd" and sem == "ev_" + engine
                    and val <= self.counts.get(sem, 0)):
                continue
            if val <= wm.get(sem, 0):
                continue
            if val > merged.get(sem, 0):
                merged[sem] = val
        ws = tuple(merged.items())
        for sem, val in ws:
            wm[sem] = val
        val = None
        if inc is not None:
            val = self.counts.get(inc, 0) + step
            self.counts[inc] = val
        self.ops.append((engine, ws, fn, inc, step))
        return (inc, val) if inc is not None else None

    def last(self, engine):
        key = "ev_" + engine
        return (key, self.counts.get(key, 0))

    def barrier(self):
        targets = []
        for e in ENGINES:
            key = "bar_" + e
            self.emit(e, None, inc=key)
            targets.append((key, self.counts[key]))
        for e in ENGINES:
            self.emit(e, None, waits=targets)

    def run(self, eng, engine_name, sems):
        for engine, waits, fn, inc, step in self.ops:
            if engine != engine_name:
                continue
            for sem_key, val in waits:
                eng.wait_ge(sems[sem_key], val)
            if fn is None:
                if inc is not None:
                    eng.sem_inc(sems[inc], step)
                continue
            instr = fn(eng)
            if inc is not None:
                if instr is not None:
                    instr.then_inc(sems[inc], step)
                else:
                    eng.sem_inc(sems[inc], step)


class Guard:

    def __init__(self):
        self.ev = {}

    def wait(self, *names):
        return [self.ev[n] for n in names if n in self.ev]

    def set(self, name, event):
        if event is not None:
            self.ev[name] = event


def build_core_program():
    nc = bass.Bass("TRN2", target_bir_lowering=False, debug=False,
                   detect_race_conditions=DETECT_RACES)

    x_in = nc.dram_tensor("x", [T, D], F32, kind="ExternalInput").ap()
    wqkv_in = nc.dram_tensor("W_qkv", [3 * D, D], F32, kind="ExternalInput").ap()
    proj_in = nc.dram_tensor("proj", [H * HD, F], F32, kind="ExternalInput").ap()
    wout_in = nc.dram_tensor("W_out", [D, D], F32, kind="ExternalInput").ap()
    bout_in = nc.dram_tensor("b_out", [1, D], F32, kind="ExternalInput").ap()
    wg_in = nc.dram_tensor("w_gate", [I, D], F32, kind="ExternalInput").ap()
    wu_in = nc.dram_tensor("w_up", [I, D], F32, kind="ExternalInput").ap()
    wd_in = nc.dram_tensor("w_down", [D, I], F32, kind="ExternalInput").ap()
    y_out = nc.dram_tensor("y", [T, D], F32, kind="ExternalOutput").ap()

    s_qkvT = nc.dram_tensor("s_qkvT", [ND, 128, 3 * D], BF16, kind="Internal").ap()
    s_outT = nc.dram_tensor("s_outT", [ND, 128, D], BF16, kind="Internal").ap()
    s_gT = nc.dram_tensor("s_gT", [ND, 128, I], BF16, kind="Internal").ap()
    s_uT = nc.dram_tensor("s_uT", [ND, 128, I], BF16, kind="Internal").ap()
    s_dT = nc.dram_tensor("s_dT", [NI, 128, D], BF16, kind="Internal").ap()
    s_x1 = nc.dram_tensor("s_x1", [NCH, 128, D], F32, kind="Internal").ap()
    s_h2q = nc.dram_tensor("s_h2q", [NCH, ND, 128, 128], BF16, kind="Internal").ap()
    s_part = nc.dram_tensor("s_part", [NCH, 128, D], F32, kind="Internal").ap()

    sch = Sched()
    G = Guard()
    esg = ExitStack()

    def sb(stack, name, shape, dt):
        return stack.enter_context(nc.sbuf_tensor(name, shape, dt))

    def ps(stack, name, shape, dt=F32):
        return stack.enter_context(nc.psum_tensor(name, shape, dt))

    ident = sb(esg, "ident", [128, 128], BF16)
    ones_bf = sb(esg, "ones_bf", [1, 128], BF16)
    projf = sb(esg, "projf", [128, ND * F], BF16)
    bout_b = sb(esg, "bout_b", [1, D], BF16)
    kv_sb = sb(esg, "kv_sb", [128, SEQ_PER_CORE * H * 65], BF16)
    ceps = sb(esg, "ceps", [128, 1], F32)
    c30 = sb(esg, "c30", [128, 1], F32)
    cln6 = sb(esg, "cln6", [128, 1], F32)
    pf = [ps(esg, f"pf{i}", [128, 512], F32) for i in range(6)]
    ptb = [ps(esg, f"ptb{i}", [128, 128], BF16) for i in range(2)]

    e_ms = sch.emit("gpsimd", lambda g: g.memset(ident[:, :], 0.0), inc="ev")
    sch.emit("gpsimd", lambda g: g.affine_select(
        out=ident[:, :], in_=ident[:, :], compare_op=ALU.not_equal,
        fill=1.0, base=0, pattern=[[-1, 128]], channel_multiplier=1),
        waits=[e_ms])
    sch.emit("gpsimd", lambda g: g.memset(ones_bf[0:1, :], 1.0))
    sch.emit("gpsimd", lambda g: g.memset(c30[:, :], 1e-30))
    sch.emit("gpsimd", lambda g: g.memset(cln6[:, :], float(np.log(6.0))))
    e_c = sch.emit("gpsimd", lambda g: g.memset(ceps[:, :], EPS), inc="ev")
    def ln_chain(x_ap, out_ap, lb, width, deps, out_war=()):
        e1 = sch.emit("scalar", lambda s: s.activation(
            out=lb["sq"][:, :width], in_=x_ap, func=ACT.Copy,
            accum_out=lb["sum1"][:, :]), waits=list(deps), inc="ev")
        e2 = sch.emit("vector", lambda v: v.tensor_single_scalar(
            out=lb["nmean"][:, :], in_=lb["sum1"][:, :], scalar=-1.0 / width,
            op=ALU.mult), waits=[e1], inc="ev")
        sch.emit("scalar", lambda s: s.activation(
            out=lb["sq"][:, :width], in_=x_ap, func=ACT.Square,
            bias=lb["nmean"][:, 0:1], accum_out=lb["varsum"][:, :]), waits=[e2])
        sch.emit("scalar", lambda s: s.activation(
            out=lb["lnv"][:, :], in_=lb["varsum"][:, :], func=ACT.Ln,
            scale=1.0 / width, bias=ceps[:, 0:1]))
        e4 = sch.emit("scalar", lambda s: s.activation(
            out=lb["rstd"][:, :], in_=lb["lnv"][:, :], func=ACT.Exp,
            scale=-0.5), inc="ev")
        e5 = sch.emit("vector", lambda v: v.scalar_tensor_tensor(
            out=lb["nmr"][:, :], in0=lb["sum1"][:, :], scalar=-1.0 / width,
            in1=lb["rstd"][:, :], op0=ALU.mult, op1=ALU.mult),
            waits=[e4], inc="ev")
        e6 = sch.emit("scalar", lambda s: s.activation(
            out=out_ap, in_=x_ap, func=ACT.Identity,
            bias=lb["nmr"][:, 0:1], scale=lb["rstd"][:, 0:1]),
            waits=[e5] + list(out_war), inc="ev")
        return e6

    THRS = ((0.75, 1.0), (1.25, 1.0), (1.75, 1.0), (2.5, 2.0), (3.5, 2.0), (5.0, 4.0))

    def quant_front(x_ap, qb, width, ptag, deps):
        nb = width // 16
        e_a = sch.emit("scalar", lambda s: s.activation(
            out=qb["a"][:, :width], in_=x_ap, func=ACT.Abs),
            waits=list(deps) + G.wait(ptag + "_a"), inc="ev")
        e_sm = sch.emit("vector", lambda v: v.tensor_reduce(
            out=qb["smax"][:, :nb],
            in_=qb["a"][:, :width].rearrange("p (b k) -> p b k", k=16),
            axis=AX.X, op=ALU.max),
            waits=[e_a] + G.wait(ptag + "_smax"), inc="ev")
        sch.emit("scalar", lambda s: s.activation(
            out=qb["lns"][:, :nb], in_=qb["smax"][:, :nb], func=ACT.Ln,
            bias=c30[:, 0:1]), waits=[e_sm])
        e_rc = sch.emit("scalar", lambda s: s.activation(
            out=qb["rcp0"][:, :nb], in_=qb["lns"][:, :nb], func=ACT.Exp,
            scale=-1.0, bias=cln6[:, 0:1]),
            waits=G.wait(ptag + "_rcp0"), inc="ev")
        G.set(ptag + "_smax", e_rc)
        return e_rc

    def quant_back(x_ap, out_ap, qb, width, ptag, stag, e_rc, out_war=()):
        nb = width // 16
        rcp_b = qb["rcp0"][:, :nb].unsqueeze(2).broadcast_to([128, nb, 16])
        sh_b = qb["shv"][:, :nb].unsqueeze(2).broadcast_to([128, nb, 16])
        e_sh = sch.emit("vector", lambda v: v.tensor_scalar(
            out=qb["shv"][:, :nb], in0=qb["smax"][:, :nb], scalar1=1.0 / 12.0,
            scalar2=5e-13, op0=ALU.mult, op1=ALU.max), inc="ev")
        e_xn = sch.emit("vector", lambda v: v.tensor_tensor(
            out=qb["xn"][:, :width].rearrange("p (b k) -> p b k", k=16),
            in0=qb["a"][:, :width].rearrange("p (b k) -> p b k", k=16),
            in1=rcp_b, op=ALU.mult), waits=[e_rc], inc="ev")
        G.set(ptag + "_a", e_xn)
        G.set(ptag + "_rcp0", e_xn)
        e_sg = sch.emit("scalar", lambda s: s.activation(
            out=qb["sg"][:, :width], in_=x_ap, func=ACT.Sign),
            waits=G.wait(stag + "_sg"), inc="ev")
        names = ["acc", "acc2"]
        cur = 0
        sch.emit("vector", lambda v: v.tensor_scalar(
            out=qb["acc"][:, :width], in0=qb["xn"][:, :width], scalar1=0.25,
            scalar2=1.0, op0=ALU.is_gt, op1=ALU.mult))
        for thr, w in THRS:
            sch.emit("vector", lambda v, thr=thr, w=w: v.tensor_scalar(
                out=qb["m"][:, :width], in0=qb["xn"][:, :width],
                scalar1=thr, scalar2=w, op0=ALU.is_gt, op1=ALU.mult))
            dst, src = names[1 - cur], names[cur]
            sch.emit("vector", lambda v, dst=dst, src=src:
                     v.tensor_tensor(
                         out=qb[dst][:, :width], in0=qb["m"][:, :width],
                         in1=qb[src][:, :width], op=ALU.add))
            cur = 1 - cur
        e_ssg = sch.emit("vector", lambda v: v.tensor_tensor(
            out=qb["ssg"][:, :width].rearrange("p (b k) -> p b k", k=16),
            in0=qb["sg"][:, :width].rearrange("p (b k) -> p b k", k=16),
            in1=sh_b, op=ALU.mult), waits=[e_sg, e_sh], inc="ev")
        G.set(stag + "_sg", e_ssg)
        e_q = sch.emit("vector", lambda v, src=names[cur]: v.tensor_tensor(
            out=out_ap, in0=qb[src][:, :width], in1=qb["ssg"][:, :width],
            op=ALU.mult), waits=list(out_war), inc="ev")
        return e_q, e_sg

    def quant_chain(x_ap, out_ap, qb, width, tag, deps, out_war=()):
        e_rc = quant_front(x_ap, qb, width, tag, deps)
        e_q, _ = quant_back(x_ap, out_ap, qb, width, tag, tag, e_rc, out_war)
        return e_q

    def elu1(psum_ap, tmp_f32, tmp_e, out_ap, deps, war=(), eltag=None):
        e1 = sch.emit("vector", lambda v: v.tensor_single_scalar(
            out=tmp_f32, in_=psum_ap, scalar=0.0, op=ALU.min),
            waits=list(deps) + list(war)
            + (G.wait(eltag) if eltag else []), inc="ev")
        e2 = sch.emit("scalar", lambda s: s.activation(
            out=tmp_e, in_=tmp_f32, func=ACT.Exp), waits=[e1], inc="ev")
        e3 = sch.emit("vector", lambda v: v.scalar_tensor_tensor(
            out=out_ap, in0=psum_ap, scalar=0.0, in1=tmp_e,
            op0=ALU.max, op1=ALU.add), waits=[e2], inc="ev")
        if eltag:
            G.set(eltag, e2)
        return e3

    es0 = ExitStack()
    projf32 = sb(es0, "projf32", [128, ND * F], F32)
    bout_f = sb(es0, "bout_f", [1, D], F32)
    e_ps = []
    for j in range(ND):
        e_ps.append(sch.emit("sync", lambda s, j=j: s.dma_start(
            out=projf32[:, j * F:(j + 1) * F], in_=proj_in[j * 128:(j + 1) * 128, :]),
            inc="dv", step=16))
    e_b = sch.emit("sync", lambda s: s.dma_start(out=bout_f[0:1, :], in_=bout_in[0:1, :]), inc="dv", step=16)
    sch.emit("scalar", lambda s: s.activation(out=projf[:, :], in_=projf32[:, :], func=ACT.Copy),
             waits=e_ps + [e_b, e_c])
    sch.emit("scalar", lambda s: s.activation(out=bout_b[0:1, :], in_=bout_f[0:1, :], func=ACT.Copy))
    def make_qset(stack, tag, width, site16, depth=2):
        dt2 = F16 if site16 else BF16
        shared = {"xn": sb(stack, f"q{tag}_xn", [128, width], F16 if site16 else F32),
                  "ssg": sb(stack, f"q{tag}_ssg", [128, width], dt2),
                  "sg": sb(stack, f"q{tag}_sg", [128, width], dt2),
                  "shv": sb(stack, f"q{tag}_shv", [128, width // 16], dt2 if site16 else BF16)}
        for nm in ("acc", "acc2", "m"):
            shared[nm] = sb(stack, f"q{tag}_" + nm, [128, width], dt2)
        if depth == 1:
            shared["a"] = sb(stack, f"q{tag}_a", [128, width], F32)
            for nm in ("smax", "lns", "rcp0"):
                shared[nm] = sb(stack, f"q{tag}_" + nm, [128, width // 16], F32)
            return shared
        group = []
        for i in range(depth):
            qq = dict(shared)
            qq["a"] = sb(stack, f"q{tag}{i}_a", [128, width], F32)
            for nm in ("smax", "lns", "rcp0"):
                qq[nm] = sb(stack, f"q{tag}{i}_" + nm, [128, width // 16], F32)
            group.append(qq)
        return group

    w_nat = [sb(es0, f"w_nat{i}", [128, I], F32) for i in range(2)]
    w_b16 = [sb(es0, f"w_b16{i}", [128, I], BF16) for i in range(2)]
    pc_tr = ptb
    tr_st = [sb(es0, f"tr_st{i}", [128, NI * 128], BF16) for i in range(2)]
    q0 = make_qset(es0, "0", I, False)

    def s0_process(src_fn, n_chunks, chunk_w, do_quant, dst_fn):
        nj = chunk_w // 128

        def s0_tr(c, bi, e_cast):
            last_tr = None
            last_cp = None
            for j in range(nj):
                pj = j % 2
                e_tr = sch.emit("tensor", lambda t, bi=bi, j=j, pj=pj: t.transpose(
                    pc_tr[pj][:, :], w_b16[bi][:, j * 128:(j + 1) * 128],
                    ident[:, :]), waits=[e_cast] + G.wait(f"ptb{pj}"),
                    inc="ev")
                last_tr = e_tr
                e_ev = sch.emit("scalar", lambda s, j=j, pj=pj, bi=bi: s.activation(
                    out=tr_st[bi][:, j * 128:(j + 1) * 128], in_=pc_tr[pj][:, :],
                    func=ACT.Copy),
                    waits=[e_tr] + (G.wait(f"tr_st{bi}") if j == 0 else []),
                    inc="ev")
                G.set(f"ptb{pj}", e_ev)
                last_cp = e_ev
            e_d = sch.emit("gpsimd", lambda g, c=c, bi=bi, nj=nj: g.dma_start(
                out=dst_fn(c, nj),
                in_=tr_st[bi][:, :nj * 128].rearrange("p (j c) -> p j c", c=128)),
                waits=[last_cp], inc="dv2", step=16)
            G.set(f"tr_st{bi}", e_d)
            G.set(f"w_b16{bi}", last_tr)

        def s0_back(c, bi, e_rc):
            e_cast, e_sg = quant_back(w_nat[bi][:, :chunk_w],
                                      w_b16[bi][:, :chunk_w], q0[bi], chunk_w,
                                      f"q0{bi}", "q0", e_rc,
                                      out_war=G.wait(f"w_b16{bi}"))
            G.set(f"w_nat{bi}", e_sg)
            s0_tr(c, bi, e_cast)

        pend = None
        for c in range(n_chunks):
            bi = c % 2
            e_ld = sch.emit("sync", lambda s, c=c, bi=bi: s.dma_start(
                out=w_nat[bi][:, :chunk_w], in_=src_fn(c)),
                waits=G.wait(f"w_nat{bi}"), inc="dv", step=16)
            if do_quant:
                e_rc = quant_front(w_nat[bi][:, :chunk_w], q0[bi], chunk_w,
                                   f"q0{bi}", [e_ld])
                if pend is not None:
                    s0_back(*pend)
                pend = (c, bi, e_rc)
            else:
                e_cast = sch.emit("scalar", lambda s, bi=bi: s.activation(
                    out=w_b16[bi][:, :chunk_w], in_=w_nat[bi][:, :chunk_w],
                    func=ACT.Copy), waits=[e_ld] + G.wait(f"w_b16{bi}"),
                    inc="ev")
                G.set(f"w_nat{bi}", e_cast)
                s0_tr(c, bi, e_cast)
        if pend is not None:
            s0_back(*pend)

    import os
    _S0SEL = globals().get("S0SEL", "all")
    _S0N = globals().get("S0_NCHUNKS", 24)
    if _S0SEL in ("all", "plain"):
        s0_process(lambda c: wqkv_in[c * 128:(c + 1) * 128, :], _S0N, D, False,
               lambda c, nj: s_qkvT[:, :, c * 128:(c + 1) * 128].rearrange("j p c -> p j c"))
    if _S0SEL == "all":
        s0_process(lambda c: wout_in[c * 128:(c + 1) * 128, :], ND, D, False,
               lambda c, nj: s_outT[:, :, c * 128:(c + 1) * 128].rearrange("j p c -> p j c"))
    if _S0SEL in ("all", "quant"):
        s0_process(lambda c: wg_in[c * 128:(c + 1) * 128, :], NI, D, True,
               lambda c, nj: s_gT[:, :, c * 128:(c + 1) * 128].rearrange("j p c -> p j c"))
    if _S0SEL == "all":
        s0_process(lambda c: wu_in[c * 128:(c + 1) * 128, :], NI, D, True,
               lambda c, nj: s_uT[:, :, c * 128:(c + 1) * 128].rearrange("j p c -> p j c"))
        s0_process(lambda c: wd_in[c * 128:(c + 1) * 128, :], ND, I, True,
               lambda c, nj: s_dT[:, :, c * 128:(c + 1) * 128].rearrange("j p c -> p j c"))

    sch.barrier()
    es0.close()
    if BUILD_STAGES == "s0":
        sch.emit("gpsimd", lambda g: g.dma_start(out=y_out[0:128, 0:64],
                 in_=ident[:, :].bitcast(F32)[:, 0:64]), inc="dv2", step=16)
        sch.emit("gpsimd", None, waits=[("dv2", sch.counts["dv2"])])
        return _finish(nc, sch, esg)

    es13 = ExitStack()
    ln1_fm = sb(es13, "ln1_fm", [128, ND * N], BF16)
    xch = [sb(es13, f"xch{i}", [128, D], F32) for i in range(2)]

    lnb_sq = sb(es13, "lnb_sq", [128, D], F32)

    def make_lnb(tag):
        lb = {nm: sb(es13, f"lnb{tag}_" + nm, [128, 1], F32)
              for nm in ("sum1", "varsum", "nmean", "lnv", "rstd", "nmr")}
        lb["sq"] = lnb_sq
        return lb

    lnb = [make_lnb(0), make_lnb(1)]
    wA_sb = sb(es13, "wA_sb", [128, ND * D], BF16)
    wB_sb = sb(es13, "wB_sb", [128, ND * D], BF16)
    k_fm = [sb(es13, f"k_fm{i}", [128, N], BF16) for i in range(2)]
    v_tm = sb(es13, "v_tm", [128, CH_SEQ * H * 65], BF16)
    kp_ch = [sb(es13, f"kp_ch{i}", [128, 512], BF16) for i in range(2)]
    el_t = [sb(es13, f"el_t{i}", [128, 512], F32) for i in range(2)]
    el_e = [sb(es13, f"el_e{i}", [128, 512], F32) for i in range(2)]
    q_fm = [sb(es13, "q_fm0", [128, ND * 512], BF16)] * 2
    qp_t = [sb(es13, f"qp_t{i}", [128, 512], BF16) for i in range(2)]
    attn_fm = [sb(es13, "attn_fm0", [128, ND * 512], BF16)] * 2
    zpool = sb(es13, "zpool", [128, 512], F32)
    zbf = [zpool[0:1, 0:256].bitcast(BF16),
           zpool[0:1, 256:512].bitcast(BF16)]
    zden = [zpool[32:33, :], zpool[64:65, :]]
    zln = zpool[96:97, :]
    zbc_sb = [sb(es13, f"zbc_sb{i}", [128, 512], BF16) for i in range(2)]
    x1_sb = [sb(es13, f"x1_sb{i}", [128, D], F32) for i in range(2)]
    h2_sb = [sb(es13, "h2_sb0", [128, D], F32)] * 2
    h2q_tm = [sb(es13, f"h2q_tm{i}", [128, D], BF16) for i in range(4)]
    ln1_tm = h2q_tm[0:2]
    h2q_st = [sb(es13, f"h2q_st{i}", [128, D], BF16) for i in range(2)]

    q3 = make_qset(es13, "3", D, False, depth=1)

    for s in range(SEQ_PER_CORE):
        pc_t1 = ptb
        for c in range(CH_SEQ):
            bi = c % 2
            gc = s * CH_SEQ + c
            e_ld = sch.emit("sync", lambda e, gc=gc, bi=bi: e.dma_start(
                out=xch[bi][:, :], in_=x_in[gc * 128:(gc + 1) * 128, :]),
                waits=G.wait(f"xch{bi}"), inc="dv", step=16)
            e_ln = ln_chain(xch[bi][:, :], ln1_tm[bi][:, :], lnb[bi], D, [e_ld],
                            out_war=G.wait(f"ln1_tm{bi}"))
            G.set(f"xch{bi}", e_ln)
            last = None
            for j in range(ND):
                pj = j % 2
                e_tr = sch.emit("tensor", lambda t, j=j, pj=pj, bi=bi: t.transpose(
                    pc_t1[pj][:, :], ln1_tm[bi][:, j * 128:(j + 1) * 128],
                    ident[:, :]), waits=[e_ln] + G.wait(f"ptb{pj}"),
                    inc="ev")
                e_ev = sch.emit("scalar", lambda sc, j=j, pj=pj, c=c: sc.activation(
                    out=ln1_fm[:, j * N + c * 128: j * N + (c + 1) * 128],
                    in_=pc_t1[pj][:, :], func=ACT.Copy),
                    waits=[e_tr], inc="ev")
                G.set(f"ptb{pj}", e_ev)
                last = e_tr
            G.set(f"ln1_tm{bi}", last)
        sch.barrier()
        if BUILD_STAGES == "s1":
            sch.emit("gpsimd", lambda g: g.dma_start(out=y_out[0:128, :],
                     in_=ln1_fm[:, 0:D]), inc="dv2", step=16)
            sch.emit("gpsimd", None, waits=[("dv2", sch.counts["dv2"])])
            es13.close()
            return _finish(nc, sch, esg)

        pc_uv = pf[0:2]
        pc_ke = pf[2:4]
        pc_kv = pf[4][:, 0:65]

        ldk = []
        for j in range(ND):
            ldk.append(sch.emit("sync", lambda e, j=j: e.dma_start(
                out=wA_sb[:, j * D:(j + 1) * D], in_=s_qkvT[j, :, D:2 * D]),
                inc="dv", step=16))
            ldk.append(sch.emit("sync", lambda e, j=j: e.dma_start(
                out=wB_sb[:, j * D:(j + 1) * D], in_=s_qkvT[j, :, 2 * D:3 * D]),
                inc="dv", step=16))
        e_vones = sch.emit("vector", lambda v: v.memset(
            v_tm[:, :].rearrange("p (c h k) -> p (c h) k", h=H, k=65)[:, :, 64:65],
            1.0), inc="ev")

        vt_ev = {}
        kf_ev = {}
        uvp = [0]

        def vt_unit(c, half, ldk=ldk):
            pi = uvp[0] % 2
            uvp[0] += 1
            for j in range(ND):
                sch.emit("tensor", lambda t, c=c, half=half, j=j, pi=pi: t.matmul(
                    pc_uv[pi][:, :],
                    ln1_fm[:, j * N + c * 128: j * N + (c + 1) * 128],
                    wB_sb[:, j * D + half * 512: j * D + (half + 1) * 512],
                    start=(j == 0), stop=(j == ND - 1)),
                    waits=(ldk + G.wait(f"pf{pi}") if j == 0 else []),
                    inc=("ev" if j == ND - 1 else None))
            e_mm = sch.last("tensor")
            e_ev = sch.emit("scalar", lambda sc, c=c, half=half, pi=pi: sc.activation(
                out=v_tm[:, c * H * 65 + half * 8 * 65: c * H * 65 + (half * 8 + 8) * 65]
                    .rearrange("p (h k) -> p h k", k=65)[:, :, 0:64],
                in_=pc_uv[pi][:, :].rearrange("p (h k) -> p h k", k=64),
                func=ACT.Copy), waits=[e_mm], inc="ev")
            G.set(f"pf{pi}", e_ev)
            vt_ev[(c, half)] = e_ev

        def kf_unit(oc, tt, ldk=ldk):
            pi = uvp[0] % 2
            uvp[0] += 1
            kbuf = k_fm[oc % 2]
            for j in range(ND):
                sch.emit("tensor", lambda t, oc=oc, tt=tt, j=j, pi=pi: t.matmul(
                    pc_uv[pi][:, :],
                    wA_sb[:, j * D + oc * 128: j * D + (oc + 1) * 128],
                    ln1_fm[:, j * N + tt * 512:j * N + (tt + 1) * 512],
                    start=(j == 0), stop=(j == ND - 1)),
                    waits=(ldk + G.wait(f"pf{pi}", f"k_fm{oc % 2}") if j == 0 else []),
                    inc=("ev" if j == ND - 1 else None))
            e_mm = sch.last("tensor")
            e_ev = sch.emit("scalar", lambda sc, tt=tt, pi=pi, kbuf=kbuf: sc.activation(
                out=kbuf[:, tt * 512:(tt + 1) * 512],
                in_=pc_uv[pi][:, :], func=ACT.Copy), waits=[e_mm], inc="ev")
            G.set(f"pf{pi}", e_ev)
            kf_ev[(oc, tt)] = e_ev

        kep = [0]

        def ke_unit_a(oc, hh, g):
            pi = kep[0] % 2
            kep[0] += 1
            kbuf = k_fm[oc % 2]
            poff = hh * 64
            for ci in range(4):
                c = g * 4 + ci
                sch.emit("tensor", lambda t, c=c, ci=ci, pi=pi, poff=poff, oc=oc, kbuf=kbuf: t.matmul(
                    pc_ke[pi][:, ci * 128:(ci + 1) * 128],
                    kbuf[poff:poff + 64, c * 128:(c + 1) * 128],
                    projf[poff:poff + 64, oc * F:(oc + 1) * F],
                    start=True, stop=True, skip_group_check=True),
                    waits=([kf_ev[(oc, g)]] + G.wait(f"pf{2+pi}") if ci == 0 else []),
                    inc=("ev" if ci == 3 else None))
            e_ke = sch.last("tensor")
            e1 = sch.emit("vector", lambda v, pi=pi: v.tensor_single_scalar(
                out=el_t[pi][:, :], in_=pc_ke[pi][:, :], scalar=0.0,
                op=ALU.min), waits=[e_ke] + G.wait(f"el{pi}"), inc="ev")
            e2 = sch.emit("scalar", lambda s, pi=pi: s.activation(
                out=el_e[pi][:, :], in_=el_t[pi][:, :], func=ACT.Exp),
                waits=[e1], inc="ev")
            G.set(f"el{pi}", e2)
            return (pi, e2, e_ke)

        def ke_unit_b(pi, e2):
            e_kp = sch.emit("vector", lambda v, pi=pi: v.scalar_tensor_tensor(
                out=kp_ch[pi][:, :], in0=pc_ke[pi][:, :], scalar=0.0,
                in1=el_e[pi][:, :], op0=ALU.max, op1=ALU.add),
                waits=[e2] + G.wait(f"kp_ch{pi}"), inc="ev")
            G.set(f"pf{2+pi}", e_kp)
            return e_kp

        def kv_unit(oc, hh, g, pi, e_kp, s=s):
            h = 2 * oc + hh
            half = h // 8
            for ci in range(4):
                c = g * 4 + ci
                w0 = []
                if ci == 0:
                    w0 = [e_kp] + [vt_ev[(g * 4 + c2, half)] for c2 in range(4)]
                    if c == 0:
                        w0 += [e_vones] + G.wait("pf4")
                sch.emit("tensor", lambda t, h=h, c=c, ci=ci, pi=pi: t.matmul(
                    pc_kv[:, :], kp_ch[pi][:, ci * 128:(ci + 1) * 128],
                    v_tm[:, c * H * 65 + h * 65: c * H * 65 + (h + 1) * 65],
                    start=(c == 0), stop=(c == CH_SEQ - 1),
                    skip_group_check=True),
                    waits=w0, inc=("ev" if ci == 3 else None))
            G.set(f"kp_ch{pi}", sch.last("tensor"))
            if g == CH_SEQ // 4 - 1:
                e_kv = sch.last("tensor")
                e_kve = sch.emit("scalar", lambda sc, s=s, h=h: sc.activation(
                    out=kv_sb[:, (s * H + h) * 65:(s * H + h + 1) * 65],
                    in_=pc_kv[:, :], func=ACT.Copy), waits=[e_kv], inc="ev")
                G.set("pf4", e_kve)

        for tt in range(4):
            kf_unit(0, tt)
        for c in range(CH_SEQ):
            vt_unit(c, 0)
        for oc in range(ND):
            if oc < 4:
                fill = ([("kf", oc + 1, t) for t in range(4)]
                        + [("vt", 4 * oc + i, 1) for i in range(4)])
            elif oc < ND - 1:
                fill = [("kf", oc + 1, t) for t in range(4)]
            else:
                fill = []
            units = [(hh, g) for hh in range(2) for g in range(CH_SEQ // 4)]
            fi = 0
            pend_a = None
            pend_b = None
            last_ke = None
            for idx in range(len(units) + 2):
                if idx < len(units):
                    hh, g = units[idx]
                    pi, e2, e_ke = ke_unit_a(oc, hh, g)
                    last_ke = e_ke
                    cur_a = (hh, g, pi, e2)
                else:
                    cur_a = None
                if pend_a is not None:
                    ahh, ag, api, ae2 = pend_a
                    e_kp = ke_unit_b(api, ae2)
                    cur_b = (ahh, ag, api, e_kp)
                else:
                    cur_b = None
                if pend_b is not None:
                    kv_unit(oc, *pend_b)
                take = (len(fill) * min(idx + 1, len(units))) // len(units)
                while fi < take:
                    kind, a, b = fill[fi]
                    fi += 1
                    if kind == "kf":
                        kf_unit(a, b)
                    else:
                        vt_unit(a, b)
                pend_a = cur_a
                pend_b = cur_b
            G.set(f"k_fm{oc % 2}", last_ke)
        sch.barrier()
        if BUILD_STAGES == "s2":
            sch.emit("gpsimd", lambda g: g.dma_start(out=y_out[0:128, 0:1024],
                     in_=kv_sb[:, 0:1024]), inc="dv2", step=16)
            sch.emit("gpsimd", None, waits=[("dv2", sch.counts["dv2"])])
            es13.close()
            return _finish(nc, sch, esg)

        pc_q = pf[0]
        pc_qe = [pf[1], pf[0]]
        pc_a = [pf[2][0:65, :], pf[3][0:65, :]]
        pc_zb = pf[4][0:64, :]
        pc_o = pf[5]

        ldq = []
        for j in range(ND):
            ldq.append(sch.emit("sync", lambda e, j=j: e.dma_start(
                out=wA_sb[:, j * D:(j + 1) * D], in_=s_qkvT[j, :, 0:D]),
                inc="dv", step=16))
            ldq.append(sch.emit("sync", lambda e, j=j: e.dma_start(
                out=wB_sb[:, j * D:(j + 1) * D], in_=s_outT[j, :, :]),
                inc="dv", step=16))

        def s3_X(tt, s=s, ldq=ldq):
            ti = tt % 2
            qfm_ev = []
            for oc in range(ND):
                for j in range(ND):
                    sch.emit("tensor", lambda t, oc=oc, tt=tt, j=j: t.matmul(
                        pc_q[:, :],
                        wA_sb[:, j * D + oc * 128: j * D + (oc + 1) * 128],
                        ln1_fm[:, j * N + tt * 512:j * N + (tt + 1) * 512],
                        start=(j == 0), stop=(j == ND - 1)),
                        waits=(ldq + G.wait("pf0", "q_fmg") if j == 0 else []),
                        inc=("ev" if j == ND - 1 else None))
                e_mm = sch.last("tensor")
                e_ev = sch.emit("scalar", lambda sc, oc=oc, ti=ti: sc.activation(
                    out=q_fm[ti][:, oc * 512:(oc + 1) * 512], in_=pc_q[:, :],
                    func=ACT.Copy), waits=[e_mm], inc="ev")
                G.set("pf0", e_ev)
                qfm_ev.append(e_ev)

            hst = {}
            e_qe_last = [None]
            e_am_last = [None]

            def s3_hA1(h):
                pi = h % 2
                poff = (h % 2) * 64
                jc = h // 2
                qek = "pf1" if pi == 0 else "pf0"
                e_qe = sch.emit("tensor", lambda t, pi=pi, poff=poff, jc=jc, ti=ti: t.matmul(
                    pc_qe[pi][:, :],
                    projf[poff:poff + 64, jc * F:(jc + 1) * F],
                    q_fm[ti][poff:poff + 64, jc * 512:(jc + 1) * 512],
                    start=True, stop=True),
                    waits=[qfm_ev[jc]] + G.wait(qek), inc="ev")
                e_qe_last[0] = e_qe
                e1 = sch.emit("vector", lambda v, pi=pi: v.tensor_single_scalar(
                    out=el_t[pi][:, :], in_=pc_qe[pi][:, :], scalar=0.0,
                    op=ALU.min), waits=[e_qe] + G.wait(f"el{pi}"), inc="ev")
                e2 = sch.emit("scalar", lambda s, pi=pi: s.activation(
                    out=el_e[pi][:, :], in_=el_t[pi][:, :], func=ACT.Exp),
                    waits=[e1], inc="ev")
                G.set(f"el{pi}", e2)
                hst[("el", h)] = e2

            def s3_hA2(h):
                pi = h % 2
                qek = "pf1" if pi == 0 else "pf0"
                e2 = hst.pop(("el", h))
                e_qp = sch.emit("vector", lambda v, pi=pi: v.scalar_tensor_tensor(
                    out=qp_t[pi][:, :], in0=pc_qe[pi][:, :], scalar=0.0,
                    in1=el_e[pi][:, :], op0=ALU.max, op1=ALU.add),
                    waits=[e2] + G.wait(f"qp_t{pi}"), inc="ev")
                G.set(qek, e_qp)
                hst[("qp", h)] = e_qp

            def s3_hB1(h):
                pi = h % 2
                e_qp = hst.pop(("qp", h))
                e_at = sch.emit("tensor", lambda t, s=s, h=h, pi=pi: t.matmul(
                    pc_a[pi][:, :],
                    kv_sb[:, (s * H + h) * 65:(s * H + h + 1) * 65],
                    qp_t[pi][:, :], start=True, stop=True),
                    waits=[e_qp] + G.wait(f"pf{2+pi}"), inc="ev")
                G.set(f"qp_t{pi}", e_at)
                e_dn = sch.emit("vector", lambda v, pi=pi: v.tensor_single_scalar(
                    out=zden[pi][0:1, :], in_=pc_a[pi][64:65, :], scalar=1e-8,
                    op=ALU.add), waits=[e_at] + G.wait(f"zden{pi}"), inc="ev")
                hst[("dn", h)] = e_dn

            def s3_hB2(h):
                pi = h % 2
                e_dn = hst.pop(("dn", h))
                e_zl = sch.emit("scalar", lambda sc, pi=pi: sc.activation(
                    out=zln[0:1, :], in_=zden[pi][0:1, :], func=ACT.Ln),
                    waits=[e_dn], inc="ev")
                G.set(f"zden{pi}", e_zl)
                e_zb = sch.emit("scalar", lambda sc, pi=pi: sc.activation(
                    out=zbf[pi][0:1, :], in_=zln[0:1, :], func=ACT.Exp,
                    scale=-1.0), waits=G.wait(f"zbf{pi}"), inc="ev")
                hst[("zb", h)] = e_zb

            def s3_hC(h):
                pi = h % 2
                poff = (h % 2) * 64
                jc = h // 2
                e_zb = hst.pop(("zb", h))
                e_zm = sch.emit("tensor", lambda t, pi=pi: t.matmul(
                    pc_zb[:, :], ones_bf[0:1, 0:64], zbf[pi][0:1, :],
                    start=True, stop=True),
                    waits=[e_zb] + G.wait("pf4"), inc="ev")
                G.set(f"zbf{pi}", e_zm)
                e_ze = sch.emit("scalar", lambda sc, pi=pi: sc.activation(
                    out=zbc_sb[pi][0:64, :], in_=pc_zb[:, :], func=ACT.Copy),
                    waits=[e_zm] + G.wait(f"zbc_sb{pi}"), inc="ev")
                G.set("pf4", e_ze)
                e_am = sch.emit("vector", lambda v, pi=pi, poff=poff, jc=jc, ti=ti: v.tensor_tensor(
                    out=attn_fm[ti][poff:poff + 64, jc * 512:(jc + 1) * 512],
                    in0=pc_a[pi][0:64, :], in1=zbc_sb[pi][0:64, :], op=ALU.mult),
                    waits=[e_ze] + G.wait("attn_fmg"), inc="ev")
                G.set(f"pf{2+pi}", e_am)
                G.set(f"zbc_sb{pi}", e_am)
                e_am_last[0] = e_am

            for k in range(H + 4):
                if k < H:
                    s3_hA1(k)
                if 1 <= k <= H:
                    s3_hA2(k - 1)
                if k >= 4:
                    s3_hC(k - 4)
                if 2 <= k <= H + 1:
                    s3_hB1(k - 2)
                if 3 <= k <= H + 2:
                    s3_hB2(k - 3)
            G.set("q_fmg", e_qe_last[0])
            return e_am_last[0]

        def s3_P3b_chunk(tt, cc, s=s):
            gc = s * CH_SEQ + tt * 4 + cc
            pp = cc % 2
            e_h2q = st3.pop((tt, cc))
            last3 = None
            last_cp = None
            for j in range(ND):
                pj = j % 2
                e_tr = sch.emit("tensor", lambda t, j=j, pj=pj, cc=cc: t.transpose(
                    ptb[pj][:, :], h2q_tm[cc][:, j * 128:(j + 1) * 128],
                    ident[:, :]), waits=[e_h2q] + G.wait(f"ptb{pj}"),
                    inc="ev")
                e_ev2 = sch.emit("scalar", lambda sc, j=j, pj=pj, pp=pp: sc.activation(
                    out=h2q_st[pp][:, j * 128:(j + 1) * 128], in_=ptb[pj][:, :],
                    func=ACT.Copy),
                    waits=[e_tr] + (G.wait(f"h2q_st{pp}") if j == 0 else []),
                    inc="ev")
                G.set(f"ptb{pj}", e_ev2)
                last3 = e_tr
                last_cp = e_ev2
            e_d2 = sch.emit("gpsimd", lambda g, gc=gc, pp=pp: g.dma_start(
                out=s_h2q[gc, :, :, :].rearrange("j p c -> p j c"),
                in_=h2q_st[pp][:, :].rearrange("p (j c) -> p j c", c=128)),
                waits=[last_cp], inc="dv2", step=16)
            G.set(f"h2q_st{pp}", e_d2)
            G.set(f"h2q_tm{cc}", last3)

        def s3_P3a(tt, e_attn, prev_tt, s=s):
            ti = tt % 2
            last_wout = None
            for cc in range(4):
                if prev_tt is not None:
                    s3_P3b_chunk(prev_tt, cc)
                gc = s * CH_SEQ + tt * 4 + cc
                bi = cc % 2
                e_xl = sch.emit("sync", lambda e, gc=gc, bi=bi: e.dma_start(
                    out=xch[bi][:, :], in_=x_in[gc * 128:(gc + 1) * 128, :]),
                    waits=G.wait(f"xch{bi}"), inc="dv", step=16)
                e_x1_parts = []
                for half in range(2):
                    pok = (cc * 2 + half) % 2
                    po = pf[5 - pok]
                    pokey = f"pf{5 - pok}"
                    for j in range(ND):
                        sch.emit("tensor", lambda t, cc=cc, half=half, j=j, ti=ti, po=po: t.matmul(
                            po[:, :],
                            attn_fm[ti][:, j * 512 + cc * 128: j * 512 + (cc + 1) * 128],
                            wB_sb[:, j * D + half * 512: j * D + (half + 1) * 512],
                            start=(j == 0), stop=False),
                            waits=([e_attn] + G.wait(pokey) if j == 0 else []))
                    sch.emit("tensor", lambda t, half=half, po=po: t.matmul(
                        po[:, :], ones_bf[0:1, :],
                        bout_b[0:1, half * 512:(half + 1) * 512],
                        start=False, stop=True), inc="ev")
                    e_ob = sch.last("tensor")
                    e_tt = sch.emit("vector", lambda v, half=half, bi=bi, po=po: v.tensor_tensor(
                        out=x1_sb[bi][:, half * 512:(half + 1) * 512],
                        in0=po[:, :],
                        in1=xch[bi][:, half * 512:(half + 1) * 512],
                        op=ALU.add),
                        waits=[e_ob, e_xl] + G.wait(f"x1_sb{bi}", f"x1d{bi}"),
                        inc="ev")
                    G.set(pokey, e_tt)
                    e_x1_parts.append(e_tt)
                last_wout = e_x1_parts[-1]
                G.set(f"xch{bi}", last_wout)
                e_x1d = sch.emit("gpsimd", lambda g, gc=gc, bi=bi: g.dma_start(
                    out=s_x1[gc, :, :], in_=x1_sb[bi][:, :]), waits=e_x1_parts,
                    inc="dv2", step=16)
                G.set(f"x1d{bi}", e_x1d)
                e_ln2 = ln_chain(x1_sb[bi][:, :], h2_sb[bi][:, :], lnb[bi], D,
                                 e_x1_parts)
                G.set(f"x1_sb{bi}", e_ln2)
                e_h2q = quant_chain(h2_sb[bi][:, :], h2q_tm[cc][:, :], q3, D,
                                    "q3", [e_ln2],
                                    out_war=G.wait(f"h2q_tm{cc}"))
                st3[(tt, cc)] = e_h2q
            G.set("attn_fmg", last_wout)

        st3 = {}
        e_attn = s3_X(0)
        s3_P3a(0, e_attn, None)
        for tt in range(1, 4):
            e_attn = s3_X(tt)
            s3_P3a(tt, e_attn, tt - 1)
        for cc in range(4):
            s3_P3b_chunk(3, cc)
        sch.barrier()
        G.ev.pop("q_fmg", None)
        if BUILD_STAGES == "s3":
            break
    es13.close()
    if BUILD_STAGES == "s3":
        sch.emit("gpsimd", None, waits=[("dv2", sch.counts["dv2"])])
        return _finish(nc, sch, esg)

    es4 = ExitStack()
    tg_sb = sb(es4, "tg_sb", [128, ND * IPANEL], BF16)
    tu_sb = sb(es4, "tu_sb", [128, ND * IPANEL], BF16)
    td_sb = sb(es4, "td_sb", [128, NIP * D], BF16)
    h2q_in = [sb(es4, f"h2q_in{i}", [128, ND * 128], BF16) for i in range(2)]
    res_in = [sb(es4, f"res_in{i}", [128, D], F32) for i in range(4)]
    h_sb = [sb(es4, f"h_sb{i}", [128, IPANEL], F32) for i in range(3)]
    silu_sb = [sb(es4, f"silu_sb{i}", [128, 512], F32) for i in range(2)]
    hq_tm = [sb(es4, f"hq_tm{i}", [128, IPANEL], BF16) for i in range(2)]
    hq_fm = [sb(es4, f"hq_fm{i}", [128, NIP * 128], BF16) for i in range(2)]
    out_st = [sb(es4, f"out_st{i}", [128, D], F32) for i in range(2)]

    q4 = make_qset(es4, "4", IPANEL, True, depth=3)
    pc_g = [pf[0], pf[2]]
    pc_u = [pf[1], pf[3]]
    pc_t4 = ptb
    pc_d = [pf[4], pf[5]]

    ITILES = (512, 512, 384)

    for panel in range(2):
        ip0 = panel * IPANEL
        ldw = []
        for j in range(ND):
            ldw.append(sch.emit("sync", lambda e, j=j, ip0=ip0: e.dma_start(
                out=tg_sb[:, j * IPANEL:(j + 1) * IPANEL],
                in_=s_gT[j, :, ip0:ip0 + IPANEL]), inc="dv", step=16))
            ldw.append(sch.emit("sync", lambda e, j=j, ip0=ip0: e.dma_start(
                out=tu_sb[:, j * IPANEL:(j + 1) * IPANEL],
                in_=s_uT[j, :, ip0:ip0 + IPANEL]), inc="dv", step=16))
        for i in range(NIP):
            ldw.append(sch.emit("sync", lambda e, i=i, panel=panel: e.dma_start(
                out=td_sb[:, i * D:(i + 1) * D],
                in_=s_dT[panel * NIP + i, :, :]), inc="dv", step=16))

        st4 = {}
        st4b = {}

        def s4_front(gc, panel=panel, ldw=ldw):
            bi = gc % 2
            qi = gc % 3
            ri = gc % 4
            e_hin = sch.emit("sync", lambda e, gc=gc, bi=bi: e.dma_start(
                out=h2q_in[bi][:, :].rearrange("p (j c) -> p j c", c=128),
                in_=s_h2q[gc, :, :, :].rearrange("j p c -> p j c")),
                waits=G.wait(f"h2q_in{bi}"), inc="dv", step=16)
            e_res = sch.emit("sync", lambda e, gc=gc, ri=ri, panel=panel: e.dma_start(
                out=res_in[ri][:, :],
                in_=(s_x1[gc, :, :] if panel == 0 else s_part[gc, :, :])),
                waits=G.wait(f"res_in{ri}"), inc="dv", step=16)
            off = 0
            e_h = None
            for it, itw in enumerate(ITILES):
                gi = bi
                for j in range(ND):
                    sch.emit("tensor", lambda t, j=j, bi=bi, off=off, itw=itw, gi=gi: t.matmul(
                        pc_g[gi][:, :itw], h2q_in[bi][:, j * 128:(j + 1) * 128],
                        tg_sb[:, j * IPANEL + off: j * IPANEL + off + itw],
                        start=(j == 0), stop=(j == ND - 1)),
                        waits=(ldw + [e_hin] + G.wait(f"pg{gi}") if j == 0 else []),
                        inc=("ev" if j == ND - 1 else None))
                e_g = sch.last("tensor")
                for j in range(ND):
                    sch.emit("tensor", lambda t, j=j, bi=bi, off=off, itw=itw, gi=gi: t.matmul(
                        pc_u[gi][:, :itw], h2q_in[bi][:, j * 128:(j + 1) * 128],
                        tu_sb[:, j * IPANEL + off: j * IPANEL + off + itw],
                        start=(j == 0), stop=(j == ND - 1)),
                        waits=(G.wait(f"pu{gi}") if j == 0 else []),
                        inc=("ev" if j == ND - 1 else None))
                e_u = sch.last("tensor")
                e_si = sch.emit("scalar", lambda sc, itw=itw, bi=bi, gi=gi: sc.activation(
                    out=silu_sb[bi][:, :itw], in_=pc_g[gi][:, :itw], func=ACT.Silu),
                    waits=[e_g] + G.wait(f"silu_sb{bi}"), inc="ev")
                G.set(f"pg{gi}", e_si)
                e_hm = sch.emit("vector", lambda v, off=off, itw=itw, bi=bi, qi=qi, gi=gi: v.tensor_tensor(
                    out=h_sb[qi][:, off:off + itw], in0=silu_sb[bi][:, :itw],
                    in1=pc_u[gi][:, :itw], op=ALU.mult),
                    waits=[e_si, e_u] + G.wait(f"h_sb{qi}"), inc="ev")
                G.set(f"pu{gi}", e_hm)
                G.set(f"silu_sb{bi}", e_hm)
                e_h = e_hm
                off += itw
            G.set(f"h2q_in{bi}", sch.last("tensor"))
            e_rc = quant_front(h_sb[qi][:, :], q4[qi], IPANEL, f"q4{qi}", [e_h])
            st4[gc] = (e_rc, e_res)

        def s4_back(gc):
            bi = gc % 2
            qi = gc % 3
            e_rc, e_res = st4.pop(gc)
            e_hq, _ = quant_back(h_sb[qi][:, :], hq_tm[bi][:, :], q4[qi],
                                 IPANEL, f"q4{qi}", "q4", e_rc,
                                 out_war=G.wait(f"hq_tm{bi}"))
            G.set(f"h_sb{qi}", e_hq)
            st4b[gc] = (e_hq, e_res)

        def s4_out(gc, panel=panel):
            bi = gc % 2
            qi = gc % 3
            ri = gc % 4
            e_hq, e_res = st4b.pop(gc)
            last_tr4 = None
            for i in range(NIP):
                pj = i % 2
                e_tr = sch.emit("tensor", lambda t, i=i, pj=pj, bi=bi: t.transpose(
                    pc_t4[pj][:, :], hq_tm[bi][:, i * 128:(i + 1) * 128],
                    ident[:, :]), waits=[e_hq] + G.wait(f"ptb{pj}"),
                    inc="ev")
                e_ev = sch.emit("scalar", lambda sc, i=i, pj=pj, bi=bi: sc.activation(
                    out=hq_fm[bi][:, i * 128:(i + 1) * 128], in_=pc_t4[pj][:, :],
                    func=ACT.Copy), waits=[e_tr] + G.wait(f"hq_fm{bi}"), inc="ev")
                G.set(f"ptb{pj}", e_ev)
                last_tr4 = e_tr
            G.set(f"hq_tm{bi}", last_tr4)
            e_hqf = sch.last("scalar")
            e_outs = []
            for half in range(2):
                for i in range(NIP):
                    sch.emit("tensor", lambda t, half=half, i=i, bi=bi: t.matmul(
                        pc_d[bi][:, :], hq_fm[bi][:, i * 128:(i + 1) * 128],
                        td_sb[:, i * D + half * 512: i * D + (half + 1) * 512],
                        start=(i == 0), stop=(i == NIP - 1)),
                        waits=([e_hqf] + G.wait(f"pf{4+bi}") if i == 0 else []),
                        inc=("ev" if i == NIP - 1 else None))
                e_d = sch.last("tensor")
                e_o = sch.emit("vector", lambda v, half=half, bi=bi, ri=ri: v.tensor_tensor(
                    out=out_st[bi][:, half * 512:(half + 1) * 512],
                    in0=pc_d[bi][:, :],
                    in1=res_in[ri][:, half * 512:(half + 1) * 512],
                    op=ALU.add),
                    waits=[e_d, e_res] + G.wait(f"out_st{bi}"), inc="ev")
                G.set(f"pf{4+bi}", e_o)
                e_outs.append(e_o)
            G.set(f"hq_fm{bi}", sch.last("tensor"))
            G.set(f"res_in{ri}", e_outs[-1])
            if panel == 0:
                e_od = sch.emit("gpsimd", lambda g, gc=gc, bi=bi: g.dma_start(
                    out=s_part[gc, :, :], in_=out_st[bi][:, :]),
                    waits=e_outs, inc="dv2", step=16)
            else:
                e_od = sch.emit("gpsimd", lambda g, gc=gc, bi=bi: g.dma_start(
                    out=y_out[gc * 128:(gc + 1) * 128, :], in_=out_st[bi][:, :]),
                    waits=e_outs, inc="dv2", step=16)
            G.set(f"out_st{bi}", e_od)

        for k in range(NCH + 3):
            if 2 <= k < NCH + 2:
                s4_back(k - 2)
            if k >= 3:
                s4_out(k - 3)
            if k < NCH:
                s4_front(k)
        sch.barrier()

    sch.emit("gpsimd", None, waits=[("dv2", sch.counts["dv2"])])
    es4.close()

    return _finish(nc, sch, esg)


def _finish(nc, sch, esg):
    sems = {}
    sem_stack = ExitStack()
    for key in set(list(sch.counts.keys()) + ["dv", "dv2"]):
        sems[key] = sem_stack.enter_context(nc.semaphore(key))
    with nc.Block() as block:
        @block.sync
        def _(e):
            sch.run(e, "sync", sems)

        @block.scalar
        def _(e):
            sch.run(e, "scalar", sems)

        @block.vector
        def _(e):
            sch.run(e, "vector", sems)

        @block.tensor
        def _(e):
            sch.run(e, "tensor", sems)

        @block.gpsimd
        def _(e):
            sch.run(e, "gpsimd", sems)
    sem_stack.close()
    esg.close()
    return nc


_NC_CACHE = [None]


def kernel(L0_attn_x, L0_attn_W_qkv, L0_attn_proj, L0_attn_W_out, L0_attn_b_out,
           L0_ffn_w_gate, L0_ffn_w_up, L0_ffn_w_down):
    x = np.ascontiguousarray(np.asarray(L0_attn_x, dtype=np.float32))
    wqkv = np.ascontiguousarray(np.asarray(L0_attn_W_qkv, dtype=np.float32))
    proj = np.ascontiguousarray(np.asarray(L0_attn_proj, dtype=np.float32)).reshape(H * HD, F)
    wout = np.ascontiguousarray(np.asarray(L0_attn_W_out, dtype=np.float32))
    bout = np.ascontiguousarray(np.asarray(L0_attn_b_out, dtype=np.float32)).reshape(1, D)
    wg = np.ascontiguousarray(np.asarray(L0_ffn_w_gate, dtype=np.float32))
    wu = np.ascontiguousarray(np.asarray(L0_ffn_w_up, dtype=np.float32))
    wd = np.ascontiguousarray(np.asarray(L0_ffn_w_down, dtype=np.float32))

    if _NC_CACHE[0] is None:
        _NC_CACHE[0] = build_core_program()
    nc = _NC_CACHE[0]
    in_maps = []
    for c in range(NCORES):
        xs = x[c * SEQ_PER_CORE:(c + 1) * SEQ_PER_CORE].reshape(T, D)
        in_maps.append({"x": np.ascontiguousarray(xs), "W_qkv": wqkv, "proj": proj,
                        "W_out": wout, "b_out": bout, "w_gate": wg, "w_up": wu,
                        "w_down": wd})
    res = run_bass_kernel_spmd(nc, in_maps, core_ids=list(range(NCORES)))
    out = np.concatenate([r["y"].reshape(SEQ_PER_CORE, N, D) for r in res.results], axis=0)
    return out
